# version 01
# baseline (speedup 1.0000x reference)
import numpy as np
import ml_dtypes

import concourse.bass as bass
import concourse.mybir as mybir
from concourse.bass_utils import run_bass_kernel_spmd

F32 = mybir.dt.float32
BF16 = mybir.dt.bfloat16
AF = mybir.ActivationFunctionType
ALU = mybir.AluOpType
AX = mybir.AxisListType.X
BF = ml_dtypes.bfloat16

DM = 2048
TOK = 2048
HALF = 1024
FH = 5632
NFB = FH // 128
NDT = DM // 128
HEADS = 8
DH = 256
CH = 256
NCH = TOK // CH
WCNT = float(FH * DM)
EPSQ = 1e-8
EPSLN = 1e-5
KB = 1024

ENGS = ("tensor", "vector", "scalar", "gpsimd", "sync")
DEBUG = False
TESTMODE = 0

_betas = np.linspace(np.log2(1.0 / 32.0), np.log2(1.0 / 512.0), HEADS)
_g = 1.0 - np.exp2(_betas)
CD = [float(v) for v in (_g ** CH)]


class _Sem:
    __slots__ = ("h", "n")

    def __init__(self, h):
        self.h = h
        self.n = 0


class B:
    def __init__(self):
        self.nc = bass.Bass("TRN2", target_bir_lowering=False)
        self.prog = {e: [] for e in ENGS}
        self._ctxs = []

    def sem(self, name):
        ctx = self.nc.semaphore(name)
        h = ctx.__enter__()
        self._ctxs.append(ctx)
        return _Sem(h)

    def op(self, eng, build, sem=None, inc=1):
        if sem is None:
            self.prog[eng].append(build)
            return None

        def f(q, _b=build, _s=sem.h, _n=inc):
            _b(q).then_inc(_s, _n)

        self.prog[eng].append(f)
        sem.n += inc
        return (sem.h, sem.n)

    def wait(self, eng, ev):
        if ev is None:
            return
        h, v = ev
        self.prog[eng].append(lambda q, _h=h, _v=v: q.wait_ge(_h, _v))

    def chain(self, eng, sem, builds, first_ev=None):
        ev = first_ev
        for f in builds:
            if ev is not None:
                self.wait(eng, ev)
            ev = self.op(eng, f, sem=sem)
        return ev

    def barrier(self, bsem):
        for e in ENGS:
            self.op(e, lambda q: q.nop(), sem=bsem, inc=1)
        tgt = bsem.n
        for e in ENGS:
            self.wait(e, (bsem.h, tgt))

    def finish(self):
        nc = self.nc
        with nc.Block() as block:
            @block.tensor
            def _(q):
                for f in self.prog["tensor"]:
                    f(q)

            @block.vector
            def _(q):
                for f in self.prog["vector"]:
                    f(q)

            @block.scalar
            def _(q):
                for f in self.prog["scalar"]:
                    f(q)

            @block.gpsimd
            def _(q):
                for f in self.prog["gpsimd"]:
                    f(q)

            @block.sync
            def _(q):
                for f in self.prog["sync"]:
                    f(q)
        for ctx in self._ctxs:
            ctx.__exit__(None, None, None)
        return nc


def build_program():
    b = B()
    nc = b.nc

    d_x = nc.dram_tensor("xT", [DM, TOK], F32, kind="ExternalInput").ap()
    d_w1T = nc.dram_tensor("w1T", [DM, FH], F32, kind="ExternalInput").ap()
    d_w2T = nc.dram_tensor("w2T", [DM, FH], F32, kind="ExternalInput").ap()
    d_w3T = nc.dram_tensor("w3T", [FH, DM], F32, kind="ExternalInput").ap()
    d_W = {
        "q": nc.dram_tensor("Wq", [DM, DM], F32, kind="ExternalInput").ap(),
        "k": nc.dram_tensor("Wk", [DM, DM], F32, kind="ExternalInput").ap(),
        "v": nc.dram_tensor("Wv", [DM, DM], F32, kind="ExternalInput").ap(),
    }
    d_wsh = nc.dram_tensor("wsh", [3, 128, 11264], F32, kind="ExternalInput").ap()
    d_b = {
        "q": nc.dram_tensor("bq_r", [128, 16], F32, kind="ExternalInput").ap(),
        "k": nc.dram_tensor("bk_r", [128, 16], F32, kind="ExternalInput").ap(),
        "v": nc.dram_tensor("bv_r", [128, 16], F32, kind="ExternalInput").ap(),
    }
    d_dt = nc.dram_tensor("dt_tab", [CH, HEADS * CH], F32, kind="ExternalInput").ap()
    d_cs = nc.dram_tensor("cs_tab", [128, HEADS * CH], F32, kind="ExternalInput").ap()
    d_kst = nc.dram_tensor("ks_tab", [CH, DM], F32, kind="ExternalInput").ap()
    d_msk = nc.dram_tensor("cmask", [128, 2], F32, kind="ExternalInput").ap()

    d_out = nc.dram_tensor("outT", [DM, TOK], F32, kind="ExternalOutput").ap()
    d_xp = nc.dram_tensor("xp", [DM, TOK], F32, kind="ExternalOutput").ap()
    d_xpp = nc.dram_tensor("xpp", [DM, TOK], F32, kind="ExternalOutput").ap()
    _dk = "ExternalOutput" if DEBUG else "Internal"
    d_w1q = nc.dram_tensor("w1q_s", [NFB, 128, DM], BF16, kind=_dk).ap()
    d_w2q = nc.dram_tensor("w2q_s", [NFB, 128, DM], BF16, kind=_dk).ap()
    d_w3q = nc.dram_tensor("w3q_s", [NDT, 128, FH], BF16, kind=_dk).ap()
    d_sc = {
        "q": nc.dram_tensor("qT_s", [DM, TOK], BF16, kind="Internal").ap(),
        "k": nc.dram_tensor("kT_s", [DM, TOK], BF16, kind="Internal").ap(),
        "v": nc.dram_tensor("vT_s", [DM, TOK], BF16, kind="Internal").ap(),
    }
    d_ab = nc.dram_tensor("ab_s", [2, HALF], F32, kind="Internal").ap()
    d_gs = nc.dram_tensor("g_s", [1, 4], F32, kind="Internal").ap()
    d_gin = nc.dram_tensor("g_in", [128, 128], F32, kind="Internal").ap()
    d_gout = nc.dram_tensor("g_out", [128, 128], F32, kind="Internal").ap()
    d_agin = nc.dram_tensor("ag_in", [DM, DH], F32, kind="Internal").ap()
    d_agout = nc.dram_tensor("ag_out", [2 * DM, DH], F32, kind="Internal").ap()
    if DEBUG:
        d_dbg = nc.dram_tensor("dbg", [128, 64], F32, kind="ExternalOutput").ap()
        d_dbgb = nc.dram_tensor("dbgb", [128, DM], BF16, kind="ExternalOutput").ap()
        d_dbgh = nc.dram_tensor("dbgh", [2, 128, NFB * HALF], BF16,
                                kind="ExternalOutput").ap()
        d_dbga = nc.dram_tensor("dbga", [6, HALF], F32, kind="ExternalOutput").ap()
        d_dbgr = nc.dram_tensor("dbgr", [12, HALF], F32, kind="ExternalOutput").ap()
        d_dbgl = nc.dram_tensor("dbgl", [2, 128, NDT * HALF], BF16,
                                kind="ExternalOutput").ap()

    SB0 = 16 * KB

    def A(name, shape, dtype, offset):
        return nc.alloc_sbuf_tensor_at(name, shape, dtype, offset=SB0 + offset)

    t_ones = A("ones_bf", [128, 2], BF16, offset=0)
    t_onesf = A("ones_f", [128, 2], F32, offset=32)
    t_sb = A("s_b", [128, 4], F32, offset=64)
    t_msk = A("cmask_t", [128, 2], F32, offset=96)
    t_gacc = A("gacc", [128, 4], F32, offset=128)
    t_hm = A("half_m", [128, 1], F32, offset=160)
    t_hp = A("half_p", [128, 1], F32, offset=192)
    t_zero = A("zero_t", [128, 1], F32, offset=208 if False else 13 * KB)
    t_bias = {
        "q": A("bq_t", [128, 16], F32, offset=224),
        "k": A("bk_t", [128, 16], F32, offset=288),
        "v": A("bv_t", [128, 16], F32, offset=352),
    }
    t_gbuf = A("gbuf", [128, 128], F32, offset=416)
    t_grow = A("g_row", [1, 8], F32, offset=928)
    t_arow = A("a_row", [1, HALF], F32, offset=960)
    t_brow = A("b_row", [1, HALF], F32, offset=960 + 4 * HALF)
    t_trow = A("t_row", [1, HALF], F32, offset=960 + 8 * HALF)

    t_ab = A("ab_b", [128, 2 * HALF], F32, offset=14 * KB)
    t_dt = A("dt_tab_t", [128, 2 * HEADS * CH], F32, offset=14 * KB)
    t_cs = A("cs_tab_t", [128, HEADS * CH], F32, offset=30 * KB)
    t_kst = A("ks_tab_t", [128, 2 * DM], F32, offset=38 * KB)
    t_wf = A("w_f32", [128, 4 * 1024], F32, offset=22 * KB)
    t_wq = A("w_q", [128, 4 * DM], BF16, offset=38 * KB)
    t_wqb = A("w_qb", [128, 4 * DM], BF16, offset=22 * KB)
    t_w3f = A("w3_f32", [128, 22 * 128], F32, offset=22 * KB)
    t_w3q = A("w3_q", [128, 2 * NFB * 128], BF16, offset=33 * KB)
    t_wc = A("w_cast", [128, 2 * NDT * 128], BF16, offset=55 * KB)

    OFF_R1 = 63 * KB
    t_xa = A("xA", [128, NDT * HALF], F32, offset=OFF_R1)
    t_ln = A("lnTb", [128, NDT * HALF], BF16, offset=OFF_R1)
    t_Sf = A("S_f32", [128, 2 * HEADS * DH], F32, offset=OFF_R1)
    t_Sb = A("S_bf", [128, 2 * HEADS * DH], BF16, offset=OFF_R1 + 16 * KB)
    t_T = A("T_f32", [128, NDT * CH], F32, offset=OFF_R1)
    t_Tb = A("T_bf", [128, NDT * CH], BF16, offset=OFF_R1 + 16 * KB)
    t_xc = A("x_chunk", [128, NDT * CH], F32, offset=OFF_R1 + 24 * KB)
    t_x2 = A("x2_chunk", [128, NDT * CH], F32, offset=OFF_R1 + 40 * KB)

    t_hid = A("hidden", [128, NFB * HALF], BF16, offset=95 * KB)

    OFF_R2 = 127 * KB
    t_ln2 = A("ln2Tb", [128, NDT * TOK], BF16, offset=OFF_R2)
    t_gsh = A("g_shard", [128, 2 * 5632], F32, offset=OFF_R2)
    t_qc = [A("qc0", [128, NDT * CH], BF16, offset=OFF_R2),
            A("qc1", [128, NDT * CH], BF16, offset=OFF_R2 + 32 * KB)]
    t_kc = [A("kc0", [128, NDT * CH], BF16, offset=OFF_R2 + 8 * KB),
            A("kc1", [128, NDT * CH], BF16, offset=OFF_R2 + 40 * KB)]
    t_kn = [A("kn0", [128, 2 * DM], BF16, offset=OFF_R2 + 16 * KB),
            A("kn1", [128, 2 * DM], BF16, offset=OFF_R2 + 48 * KB)]
    t_vn = [A("vn0", [128, 2 * DM], BF16, offset=OFF_R2 + 24 * KB),
            A("vn1", [128, 2 * DM], BF16, offset=OFF_R2 + 56 * KB)]

    t_tmp1 = A("tmp1", [128, HALF], F32, offset=191 * KB)
    t_pe0 = A("pe0", [128, TOK], BF16, offset=191 * KB)
    t_xout = A("xout", [128, HALF], F32, offset=195 * KB)
    t_pe1 = A("pe1", [128, TOK], BF16, offset=195 * KB)
    t_stbf = A("st_bf", [128, 4 * HALF], BF16, offset=199 * KB)
    t_qsh = A("qs_h", [128, 2 * CH], BF16, offset=OFF_R1 + 56 * KB)
    t_at = A("attnS", [128, 2 * CH], BF16, offset=OFF_R1 + 57 * KB)

    ps = nc.alloc_psum_tensor("ps", [128, 4096], F32)
    psA = ps.ap()

    def pbank(i, w=512):
        return psA[:, 512 * i:512 * i + w]

    def v3(t, k, n):
        return t.ap().rearrange("p (k n) -> p k n", k=k, n=n)

    ln3 = v3(t_ln, NDT, HALF)
    ln23 = v3(t_ln2, NDT, TOK)
    xa3 = v3(t_xa, NDT, HALF)
    hid3 = v3(t_hid, NFB, HALF)
    wf3 = t_wf.ap().rearrange("p (m h n) -> p m h n", m=2, h=2, n=HALF)
    wq3 = t_wq.ap().rearrange("p (pp m n) -> p pp m n", pp=2, m=2, n=DM)
    wqb3 = t_wqb.ap().rearrange("p (pp m n) -> p pp m n", pp=2, m=2, n=DM)
    w3q3 = t_w3q.ap().rearrange("p (pp n) -> p pp n", pp=2, n=NFB * 128)
    wc3 = t_wc.ap().rearrange("p (pp n) -> p pp n", pp=2, n=NDT * 128)
    ab2 = v3(t_ab, 2, HALF)
    stbf2 = v3(t_stbf, 4, HALF)
    gsh2 = v3(t_gsh, 2, 5632)
    dt3 = v3(t_dt, 2, HEADS * CH)
    kst3 = v3(t_kst, 2, DM)
    Sf3 = t_Sf.ap().rearrange("p (k h e) -> p k h e", k=2, h=HEADS, e=DH)
    Sb3 = t_Sb.ap().rearrange("p (k h e) -> p k h e", k=2, h=HEADS, e=DH)
    T3 = t_T.ap().rearrange("p (k h e) -> p k h e", k=2, h=HEADS, e=DH)
    Tb3 = t_Tb.ap().rearrange("p (k h e) -> p k h e", k=2, h=HEADS, e=DH)
    xc3 = v3(t_xc, NDT, CH)
    x23 = v3(t_x2, NDT, CH)
    at2 = v3(t_at, 2, CH)
    qsh2 = v3(t_qsh, 2, CH)
    qc3 = [v3(t, NDT, CH) for t in t_qc]
    kc3 = [v3(t, NDT, CH) for t in t_kc]
    kn3 = [v3(t, 2, DM) for t in t_kn]
    vn3 = [v3(t, 2, DM) for t in t_vn]
    lnq = t_ln.ap()

    sT = b.sem("sT")
    sV = b.sem("sV")
    sA = b.sem("sA")
    sBar = b.sem("sBar")
    sO = b.sem("sO")
    sInit = b.sem("sInit")
    sTabs = b.sem("sTabs")
    sGM = b.sem("sGM")
    sCC = b.sem("sCC")
    sGSH = [b.sem(f"sGSH{i}") for i in range(2)]
    sWF = [b.sem(f"sWF{i}") for i in range(4)]
    sWSQ = [b.sem(f"sWSQ{i}") for i in range(4)]
    sWQB = [b.sem(f"sWQB{i}") for i in range(4)]
    sW3F = b.sem("sW3F")
    sW3SQ = [b.sem(f"sW3SQ{i}") for i in range(2)]
    sW3R = [b.sem(f"sW3R{i}") for i in range(2)]
    sXA = b.sem("sXA")
    sAB = b.sem("sAB")
    sXIN = b.sem("sXIN")
    sXO = b.sem("sXO")
    sWC = [b.sem(f"sWC{i}") for i in range(2)]
    sPE = [b.sem(f"sPE{i}") for i in range(2)]
    sQC = [b.sem(f"sQC{i}") for i in range(2)]
    sKC = [b.sem(f"sKC{i}") for i in range(2)]
    sKN = [b.sem(f"sKN{i}") for i in range(2)]
    sVN = [b.sem(f"sVN{i}") for i in range(2)]
    sXC = b.sem("sXC")
    sX2 = b.sem("sX2")
    if DEBUG:
        sDBG = b.sem("sDBG")

    E = {}

    def ph_init():
        b.op("gpsimd", lambda q: q.dma_start(out=t_msk[:], in_=d_msk), sem=sInit, inc=16)
        b.op("gpsimd", lambda q: q.dma_start(out=t_bias["q"][:], in_=d_b["q"]), sem=sInit, inc=16)
        b.op("gpsimd", lambda q: q.dma_start(out=t_bias["k"][:], in_=d_b["k"]), sem=sInit, inc=16)
        E["init"] = b.op("gpsimd", lambda q: q.dma_start(out=t_bias["v"][:], in_=d_b["v"]),
                         sem=sInit, inc=16)
        b.op("vector", lambda q: q.memset(t_ones[:], 1.0))
        b.op("vector", lambda q: q.memset(t_onesf[:], 1.0))
        b.op("vector", lambda q: q.memset(t_gbuf[:], 0.0))
        b.op("vector", lambda q: q.memset(t_hm[:], -0.5))
        b.op("vector", lambda q: q.memset(t_hp[:], 0.5))
        b.op("vector", lambda q: q.memset(t_zero[:], 0.0))
        E["zeros"] = b.op("vector", lambda q: q.memset(t_gacc[:], 0.0), sem=sV)

        red_evs = []
        for i in range(6):
            m, c = divmod(i, 2)
            if i >= 2:
                b.wait("sync", red_evs[i - 2])
            ev_d = b.op("sync", lambda q, _m=m, _c=c: q.dma_start(
                out=gsh2[:, _c, :], in_=d_wsh[_m, :, _c * 5632:(_c + 1) * 5632]),
                sem=sGSH[c], inc=16)
            if i == 0:
                b.wait("vector", E["zeros"])
            b.wait("vector", ev_d)
            ev_r = b.op("vector", lambda q, _c=c: q.reduce_sum(
                t_gacc[:, 3:4], gsh2[:, _c, :], axis=AX, apply_absolute_value=True), sem=sV)
            b.wait("vector", ev_r)
            red_evs.append(b.op("vector", lambda q, _m=m: q.tensor_tensor(
                t_gacc[:, _m:_m + 1], t_gacc[:, _m:_m + 1], t_gacc[:, 3:4], ALU.add),
                sem=sV))
        b.wait("tensor", red_evs[-1])
        ev_mm = b.op("tensor", lambda q: q.matmul(
            psA[0:1, 0:3], t_onesf[:, 0:1], t_gacc[:, 0:3], start=True, stop=True), sem=sT)
        b.wait("vector", ev_mm)
        ev_gb = b.op("vector", lambda q: q.tensor_copy(t_gbuf[0:1, 0:3], psA[0:1, 0:3]),
                     sem=sV)
        b.wait("gpsimd", ev_gb)
        ev_gi = b.op("gpsimd", lambda q: q.dma_start(out=d_gin, in_=t_gbuf[:]),
                     sem=sGM, inc=16)
        b.wait("gpsimd", ev_gi)
        ev_ar = b.op("gpsimd", lambda q: q.collective_compute(
            "AllReduce", ALU.add, replica_groups=[list(range(8))],
            ins=[d_gin.opt()], outs=[d_gout.opt()]), sem=sGM, inc=1)
        b.wait("gpsimd", ev_ar)
        ev_gr = b.op("gpsimd", lambda q: q.dma_start(out=t_grow[:], in_=d_gout[0:1, 0:8]),
                     sem=sGM, inc=16)
        b.wait("vector", ev_gr)
        ev_s = b.chain("vector", sV, [
            lambda q: q.tensor_scalar(t_grow[0:1, 4:7], t_grow[0:1, 0:3],
                                      1.0 / WCNT, EPSQ, ALU.mult, ALU.add),
            lambda q: q.reciprocal(t_grow[0:1, 0:3], t_grow[0:1, 4:7]),
        ])
        b.wait("gpsimd", ev_s)
        ev_so = b.op("gpsimd", lambda q: q.dma_start(out=d_gs[0:1, 0:3], in_=t_grow[0:1, 0:3]),
                     sem=sGM, inc=16)
        b.wait("gpsimd", ev_so)
        E["s_ready"] = b.op("gpsimd", lambda q: q.dma_start(
            out=t_sb[:, 0:3], in_=d_gs[0:1, 0:3].partition_broadcast(128)), sem=sGM, inc=16)
        if DEBUG:
            b.wait("gpsimd", E["s_ready"])
            b.op("gpsimd", lambda q: q.dma_start(out=d_dbg[:, 0:4], in_=t_sb[:]),
                 sem=sDBG, inc=16)
            b.op("gpsimd", lambda q: q.dma_start(out=d_dbg[:, 4:8], in_=t_gacc[:]),
                 sem=sDBG, inc=16)
            b.op("gpsimd", lambda q: q.dma_start(out=d_dbg[0:1, 8:16], in_=t_grow[:]),
                 sem=sDBG, inc=16)
            b.op("gpsimd", lambda q: q.dma_start(out=d_dbg[0:1, 16:24],
                                                 in_=t_gbuf[0:1, 0:8]), sem=sDBG, inc=16)

    def ln_block(x_src, half, scale, dst3, dst_c0, wait_load=()):
        c0 = half * HALF
        for eng, ev in wait_load:
            b.wait(eng, ev)
        ev_x = b.op("scalar", lambda q: q.dma_start(
            out=xa3, in_=x_src[:, c0:c0 + HALF].rearrange("(k p) t -> p k t", p=128)),
            sem=sXA, inc=16)
        mb = psA[0:1, 4 * 512:4 * 512 + HALF]
        qb = psA[0:1, 6 * 512:6 * 512 + HALF]
        b.wait("vector", ev_x)
        b.wait("scalar", ev_x)
        ev_mm = None
        mm_ev = [None] * NDT
        for k in range(NDT):
            sl = 2 * (k % 2)
            if k >= 2:
                b.wait("vector", mm_ev[k - 2])
                b.wait("scalar", mm_ev[k - 2])
            ev_cb = b.op("vector", lambda q, _k=k, _sl=sl: q.tensor_copy(
                stbf2[:, _sl, :], xa3[:, _k, :]), sem=sV)
            ev_sq = b.op("scalar", lambda q, _k=k, _sl=sl: q.activation(
                stbf2[:, _sl + 1, :], xa3[:, _k, :], AF.Square,
                bias=t_zero[:, 0:1], scale=t_onesf[:, 0:1]), sem=sA)
            b.wait("tensor", ev_cb)
            b.wait("tensor", ev_sq)
            for c in (0, 512):
                b.op("tensor", lambda q, _k=k, _c=c, _sl=sl: q.matmul(
                    mb[:, _c:_c + 512], t_ones[:, 0:1], stbf2[:, _sl, _c:_c + 512],
                    start=(_k == 0), stop=(_k == NDT - 1)))
                ev_mm = b.op("tensor", lambda q, _k=k, _c=c, _sl=sl: q.matmul(
                    qb[:, _c:_c + 512], t_ones[:, 0:1], stbf2[:, _sl + 1, _c:_c + 512],
                    start=(_k == 0), stop=(_k == NDT - 1)),
                    sem=sT if c == 512 else None)
            mm_ev[k] = ev_mm
        b.wait("vector", ev_mm)
        if DEBUG and not E.get("_lndbg"):
            ev1 = b.op("vector", lambda q: q.tensor_copy(t_trow[0:1, :], mb), sem=sV)
            b.wait("gpsimd", ev1)
            ev2 = b.op("gpsimd", lambda q: q.dma_start(out=d_dbga[2:3, :],
                                                       in_=t_trow[0:1, :]), sem=sDBG, inc=16)
            b.wait("vector", ev2)
            ev3 = b.op("vector", lambda q: q.tensor_copy(t_trow[0:1, :], qb), sem=sV)
            b.wait("gpsimd", ev3)
            ev4 = b.op("gpsimd", lambda q: q.dma_start(out=d_dbga[3:4, :],
                                                       in_=t_trow[0:1, :]), sem=sDBG, inc=16)
            b.wait("vector", ev4)
        dbg_on = DEBUG and not E.get("_lnrdbg")
        if dbg_on:
            E["_lnrdbg"] = True

        def _dump(row, src_t):
            if not dbg_on:
                return
            ev = b.op("vector", lambda q: q.nop(), sem=sV)
            b.wait("gpsimd", ev)
            evd = b.op("gpsimd", lambda q, _r=row, _s=src_t: q.dma_start(
                out=d_dbgr[_r:_r + 1, :], in_=_s[0:1, :]), sem=sGM, inc=16)
            b.wait("vector", evd)

        steps = [
            (lambda q: q.tensor_scalar_mul(t_arow[0:1, :], mb, 1.0 / DM), 0, t_arow),
            (lambda q: q.tensor_scalar_mul(t_brow[0:1, :], qb, 1.0 / DM), 1, t_brow),
            (lambda q: q.tensor_tensor(t_trow[0:1, :], t_arow[0:1, :], t_arow[0:1, :],
                                       ALU.mult), 2, t_trow),
            (lambda q: q.tensor_tensor(t_brow[0:1, :], t_brow[0:1, :], t_trow[0:1, :],
                                       ALU.subtract), 3, t_brow),
            (lambda q: q.tensor_scalar(t_trow[0:1, :], t_brow[0:1, :], 1.0, EPSLN,
                                       ALU.mult, ALU.add), 4, t_trow),
        ]
        ev1 = None
        for f, row, srct in steps:
            if ev1 is not None:
                b.wait("vector", ev1)
            ev1 = b.op("vector", f, sem=sV)
            _dump(row, srct)
        b.wait("scalar", ev1)
        ev_l = b.op("scalar", lambda q: q.activation(
            t_brow[0:1, :], t_trow[0:1, :], AF.Ln,
            bias=t_zero[0:1, 0:1], scale=t_onesf[0:1, 0:1]), sem=sA)
        b.wait("scalar", ev_l)
        ev_rs = b.op("scalar", lambda q: q.activation(
            t_trow[0:1, :], t_brow[0:1, :], AF.Exp,
            bias=t_zero[0:1, 0:1], scale=t_hm[0:1, 0:1]), sem=sA)
        b.wait("vector", ev_rs)
        _dump(6, t_trow)
        steps2 = [
            (lambda q: q.tensor_scalar_mul(t_trow[0:1, :], t_trow[0:1, :], float(scale)),
             7, t_trow),
            (lambda q: q.tensor_tensor(t_arow[0:1, :], t_arow[0:1, :], t_trow[0:1, :],
                                       ALU.mult), 8, t_arow),
            (lambda q: q.tensor_scalar_mul(t_arow[0:1, :], t_arow[0:1, :], -1.0),
             9, t_arow),
        ]
        ev2 = ev_rs
        for f, row, srct in steps2:
            b.wait("vector", ev2)
            ev2 = b.op("vector", f, sem=sV)
            _dump(row, srct)
        b.wait("scalar", ev2)
        b.op("scalar", lambda q: q.dma_start(out=d_ab[0:1, :], in_=t_trow[0:1, :]),
             sem=sAB, inc=16)
        ev_w = b.op("scalar", lambda q: q.dma_start(out=d_ab[1:2, :], in_=t_arow[0:1, :]),
                    sem=sAB, inc=16)
        b.wait("scalar", ev_w)
        b.op("scalar", lambda q: q.dma_start(
            out=ab2[:, 0, :], in_=d_ab[0:1, :].partition_broadcast(128)), sem=sAB, inc=16)
        ev_bc = b.op("scalar", lambda q: q.dma_start(
            out=ab2[:, 1, :], in_=d_ab[1:2, :].partition_broadcast(128)), sem=sAB, inc=16)
        b.wait("vector", ev_bc)
        if DEBUG and not E.get("_lndbg"):
            E["_lndbg"] = True
            b.wait("gpsimd", ev_bc)
            b.op("gpsimd", lambda q: q.dma_start(out=d_dbga[0:1, :], in_=ab2[0:1, 0, :]),
                 sem=sDBG, inc=16)
            b.op("gpsimd", lambda q: q.dma_start(out=d_dbga[1:2, :], in_=ab2[0:1, 1, :]),
                 sem=sDBG, inc=16)
            b.op("gpsimd", lambda q: q.dma_start(out=d_dbga[4:5, :], in_=t_trow[0:1, :]),
                 sem=sDBG, inc=16)
            b.op("gpsimd", lambda q: q.dma_start(out=d_dbga[5:6, :], in_=t_arow[0:1, :]),
                 sem=sDBG, inc=16)
        ev_ap = None
        for k in range(NDT):
            b.op("vector", lambda q, _k=k: q.tensor_tensor(
                t_tmp1[:], xa3[:, _k, :], ab2[:, 0, :], ALU.mult))
            ev_ap = b.op("vector", lambda q, _k=k: q.tensor_tensor(
                dst3[:, _k, dst_c0:dst_c0 + HALF], t_tmp1[:], ab2[:, 1, :], ALU.add), sem=sV)
        return ev_ap

    def x1x2(kind, ev_ln, wait_w=(), gate_reads=()):
        grp_ev = [None] * NFB
        ev_evac_v = [None] * NFB
        ev_evac_s = [None] * NFB
        ev_q2 = {}
        ev_sdma = {}
        prev_add = None
        wsrc = {0: d_w1T, 1: d_w2T}
        wdst = {0: d_w1q, 1: d_w2q}
        for fb in range(NFB):
            par = fb % 2
            wv = wq3 if kind == "quant" else wqb3
            if kind == "quant":
                for m in (0, 1):
                    for hh in (0, 1):
                        sl = 2 * m + hh
                        if fb > 0:
                            b.wait("sync", ev_q2[(fb - 1, m, hh)])
                        elif m == 0 and hh == 0:
                            for ev in wait_w:
                                b.wait("sync", ev)
                        ev_d = b.op("sync", lambda q, _m=m, _hh=hh, _fb=fb: q.dma_start(
                            out=wf3[:, _m, _hh, :].rearrange("p (k f) -> p k f", k=8),
                            in_=wsrc[_m][_hh * 1024:(_hh + 1) * 1024,
                                         _fb * 128:(_fb + 1) * 128]
                            .rearrange("(k p) f -> p k f", p=128)),
                            sem=sWF[sl], inc=16)
                        if fb == 0 and m == 0 and hh == 0:
                            b.wait("scalar", E["s_ready"])
                            b.wait("scalar", ev_ln)
                        if fb >= 2:
                            b.wait("scalar", grp_ev[fb - 2])
                            b.wait("scalar", ev_sdma[(fb - 2, m)])
                        b.wait("scalar", ev_d)
                        b.op("scalar", lambda q, _par=par, _m=m, _hh=hh: q.activation(
                            wq3[:, _par, _m, _hh * 1024:(_hh + 1) * 1024],
                            wf3[:, _m, _hh, :], AF.Sign,
                            bias=t_hm[:, 0:1], scale=t_sb[:, _m:_m + 1]))
                        b.wait("scalar", prev_add)
                        ev_o2 = b.op("scalar", lambda q, _m=m, _hh=hh: q.activation(
                            stbf2[:, 0, :], wf3[:, _m, _hh, :], AF.Sign,
                            bias=t_hp[:, 0:1], scale=t_sb[:, _m:_m + 1]), sem=sA)
                        b.wait("vector", ev_o2)
                        ev_q2[(fb, m, hh)] = b.op(
                            "vector", lambda q, _par=par, _m=m, _hh=hh: q.tensor_tensor(
                                wq3[:, _par, _m, _hh * 1024:(_hh + 1) * 1024],
                                wq3[:, _par, _m, _hh * 1024:(_hh + 1) * 1024],
                                stbf2[:, 0, :], ALU.add), sem=sV)
                        prev_add = ev_q2[(fb, m, hh)]
                    b.wait("scalar", ev_q2[(fb, m, 1)])
                    ev_sdma[(fb, m)] = b.op(
                        "scalar", lambda q, _par=par, _m=m, _fb=fb: q.dma_start(
                            out=wdst[_m][_fb], in_=wq3[:, _par, _m, :]),
                        sem=sWSQ[2 * par + m], inc=16)
                    if DEBUG and fb == 0 and m == 0:
                        b.wait("gpsimd", ev_q2[(fb, m, 1)])
                        b.op("gpsimd", lambda q: q.dma_start(
                            out=d_dbgb, in_=wq3[:, 0, 0, :]), sem=sDBG, inc=16)
                ev_wready = ev_q2[(fb, 1, 1)]
            else:
                ev_wready = None
                for m in (0, 1):
                    if fb >= 2:
                        b.wait("sync", grp_ev[fb - 2])
                    elif fb == 0 and m == 0:
                        for ev in wait_w:
                            b.wait("sync", ev)
                        for ev in gate_reads:
                            b.wait("sync", ev)
                    ev_wready = b.op("sync", lambda q, _par=par, _m=m, _fb=fb: q.dma_start(
                        out=wqb3[:, _par, _m, :], in_=wdst[_m][_fb]),
                        sem=sWQB[2 * par + m], inc=16)

            b.wait("tensor", ev_wready)
            if fb == 0:
                b.wait("tensor", ev_ln)
            if fb >= 2:
                b.wait("tensor", ev_evac_v[fb - 2])
                b.wait("tensor", ev_evac_s[fb - 2])
            x1b = (0, 1) if par == 0 else (4, 5)
            x2b = (2, 3) if par == 0 else (6, 7)
            ev = None
            for ci, c in enumerate((0, 512)):
                for k in range(NDT):
                    b.op("tensor", lambda q, _par=par, _k=k, _c=c, _b0=x1b[ci], _wv=wv:
                         q.matmul(pbank(_b0), _wv[:, _par, 0, _k * 128:(_k + 1) * 128],
                                  ln3[:, _k, _c:_c + 512],
                                  start=(_k == 0), stop=(_k == NDT - 1)))
                    last = (ci == 1 and k == NDT - 1)
                    ev = b.op("tensor", lambda q, _par=par, _k=k, _c=c, _b0=x2b[ci], _wv=wv:
                              q.matmul(pbank(_b0), _wv[:, _par, 1, _k * 128:(_k + 1) * 128],
                                       ln3[:, _k, _c:_c + 512],
                                       start=(_k == 0), stop=(_k == NDT - 1)),
                              sem=sT if last else None)
            grp_ev[fb] = ev
            b.wait("scalar", grp_ev[fb])
            ev_evac_s[fb] = b.op("scalar", lambda q, _b0=x1b[0]: q.activation(
                t_tmp1[:], psA[:, _b0 * 512:_b0 * 512 + 1024], AF.Sigmoid,
                bias=t_zero[:, 0:1], scale=t_onesf[:, 0:1]), sem=sA)
            b.wait("vector", ev_evac_s[fb])
            ev_evac_v[fb] = b.op("vector", lambda q, _b0=x2b[0], _fb=fb:
                                 q.scalar_tensor_tensor(
                hid3[:, _fb, :], psA[:, _b0 * 512:_b0 * 512 + 1024], 0.25,
                t_tmp1[:], ALU.mult, ALU.mult), sem=sV)
        return ev_evac_v[NFB - 1]

    def w3_stage(kind, half, x_src, x_dst, ev_hid, gate_reads=()):
        c0 = half * HALF
        grp_ev = [None] * NDT
        ev_ev = [None] * NDT
        ev_xodma = [None] * NDT
        ev_q2 = {}
        ev_w3dma = [None] * NDT
        prev_add = None
        osem = sO if x_dst is d_out else sXO
        for eb in range(NDT):
            par = eb % 2
            if kind == "quant":
                for hh in (0, 1):
                    if eb == 0 and hh == 0:
                        b.wait("sync", ev_hid)
                    else:
                        b.wait("sync", ev_q2[(eb, 0)] if hh == 1
                               else ev_q2[(eb - 1, 1)])
                    ev_d = b.op("sync", lambda q, _eb=eb, _hh=hh: q.dma_start(
                        out=t_w3f[:].rearrange("p (k e) -> p k e", k=22),
                        in_=d_w3T[_hh * 2816:(_hh + 1) * 2816, _eb * 128:(_eb + 1) * 128]
                        .rearrange("(k p) e -> p k e", p=128)), sem=sW3F, inc=16)
                    if eb >= 2 and hh == 0:
                        b.wait("scalar", grp_ev[eb - 2])
                        b.wait("scalar", ev_w3dma[eb - 2])
                    b.wait("scalar", ev_d)
                    b.op("scalar", lambda q, _par=par, _hh=hh: q.activation(
                        w3q3[:, _par, _hh * 2816:(_hh + 1) * 2816], t_w3f[:], AF.Sign,
                        bias=t_hm[:, 0:1], scale=t_sb[:, 2:3]))
                    b.wait("scalar", prev_add)
                    ev_o2 = b.op("scalar", lambda q: q.activation(
                        lnq[:, 0:2816], t_w3f[:], AF.Sign,
                        bias=t_hp[:, 0:1], scale=t_sb[:, 2:3]), sem=sA)
                    b.wait("vector", ev_o2)
                    ev_q2[(eb, hh)] = b.op("vector", lambda q, _par=par, _hh=hh:
                                           q.tensor_tensor(
                        w3q3[:, _par, _hh * 2816:(_hh + 1) * 2816],
                        w3q3[:, _par, _hh * 2816:(_hh + 1) * 2816],
                        lnq[:, 0:2816], ALU.add), sem=sV)
                    prev_add = ev_q2[(eb, hh)]
                b.wait("scalar", ev_q2[(eb, 1)])
                ev_w3dma[eb] = b.op("scalar", lambda q, _par=par, _eb=eb: q.dma_start(
                    out=d_w3q[_eb], in_=w3q3[:, _par, :]), sem=sW3SQ[par], inc=16)
                ev_wready = ev_q2[(eb, 1)]
            else:
                if eb >= 2:
                    b.wait("sync", grp_ev[eb - 2])
                elif eb == 0:
                    b.wait("sync", ev_hid)
                    for ev in gate_reads:
                        b.wait("sync", ev)
                ev_wready = b.op("sync", lambda q, _par=par, _eb=eb: q.dma_start(
                    out=w3q3[:, _par, :], in_=d_w3q[_eb]), sem=sW3R[par], inc=16)
            if eb == 0:
                b.wait("scalar", ev_hid)
            else:
                b.wait("scalar", ev_ev[eb - 1])
            ev_xin = b.op("scalar", lambda q, _eb=eb: q.dma_start(
                out=t_tmp1[:], in_=x_src[_eb * 128:(_eb + 1) * 128, c0:c0 + HALF]),
                sem=sXIN, inc=16)
            b.wait("tensor", ev_wready)
            if eb == 0:
                b.wait("tensor", ev_hid)
            if eb >= 2:
                b.wait("tensor", ev_ev[eb - 2])
            bk = 0 if par == 0 else 2
            for ci, c in enumerate((0, 512)):
                for k in range(NFB):
                    last = (ci == 1 and k == NFB - 1)
                    grp_ev[eb] = b.op("tensor", lambda q, _par=par, _k=k, _c=c, _bk=bk + ci:
                                      q.matmul(pbank(_bk),
                                               w3q3[:, _par, _k * 128:(_k + 1) * 128],
                                               hid3[:, _k, _c:_c + 512],
                                               start=(_k == 0), stop=(_k == NFB - 1)),
                                      sem=sT if last else None)
            b.wait("vector", grp_ev[eb])
            b.wait("vector", ev_xin)
            if eb > 0:
                b.wait("vector", ev_xodma[eb - 1])
            ev_ev[eb] = b.op("vector", lambda q, _bk=bk: q.tensor_tensor(
                t_xout[:], psA[:, _bk * 512:_bk * 512 + 1024], t_tmp1[:], ALU.add), sem=sV)
            b.wait("scalar", ev_ev[eb])
            ev_xodma[eb] = b.op("scalar", lambda q, _eb=eb: q.dma_start(
                out=x_dst[_eb * 128:(_eb + 1) * 128, c0:c0 + HALF], in_=t_xout[:]),
                sem=osem, inc=16)
        return ev_ev[NDT - 1], ev_xodma[NDT - 1], grp_ev[NDT - 1]

    def proj(ev_ln2_done):
        ev_grp = {}
        ev_evac = {}
        ev_edma = {}
        mats = ("q", "k", "v")
        seq = [(mm_, eb) for mm_ in mats for eb in range(NDT)]
        for i, (mm_, eb) in enumerate(seq):
            par = i % 2
            if i >= 2:
                b.wait("gpsimd", ev_grp[seq[i - 2]])
            ev_w = b.op("gpsimd", lambda q, _m=mm_, _eb=eb, _par=par: q.dma_start(
                out=wc3[:, _par, :].rearrange("p (k e) -> p k e", k=16),
                in_=d_W[_m][:, _eb * 128:(_eb + 1) * 128]
                .rearrange("(k p) e -> p k e", p=128)), sem=sWC[par], inc=16)
            b.wait("tensor", ev_w)
            if i == 0:
                b.wait("tensor", ev_ln2_done)
            if i >= 2:
                b.wait("tensor", ev_evac[seq[i - 2]])
            b0 = 0 if par == 0 else 4
            ev = None
            for ci in range(4):
                for k in range(NDT):
                    last = (ci == 3 and k == NDT - 1)
                    ev = b.op("tensor", lambda q, _par=par, _k=k, _ci=ci, _b0=b0:
                              q.matmul(pbank(_b0 + _ci),
                                       wc3[:, _par, _k * 128:(_k + 1) * 128],
                                       ln23[:, _k, _ci * 512:(_ci + 1) * 512],
                                       start=(_k == 0), stop=(_k == NDT - 1)),
                              sem=sT if last else None)
            ev_grp[(mm_, eb)] = ev
            b.wait("vector", ev)
            if i == 0:
                b.wait("vector", E["init"])
            if i >= 2:
                b.wait("vector", ev_edma[seq[i - 2]])
            pe = t_pe0 if par == 0 else t_pe1
            ev_evac[(mm_, eb)] = b.op("vector", lambda q, _m=mm_, _eb=eb, _pe=pe, _b0=b0:
                                      q.tensor_scalar_add(
                _pe[:], psA[:, _b0 * 512:(_b0 + 4) * 512],
                t_bias[_m][:, _eb:_eb + 1]), sem=sV)
            b.wait("scalar", ev_evac[(mm_, eb)])
            ev_edma[(mm_, eb)] = b.op("scalar", lambda q, _m=mm_, _eb=eb, _pe=pe:
                                      q.dma_start(
                out=d_sc[_m][_eb * 128:(_eb + 1) * 128, :], in_=_pe[:]), sem=sPE[par], inc=16)
        E["proj_last_mm"] = ev_grp[seq[-1]]
        E["proj_pe0"] = (sPE[0].h, sPE[0].n)
        E["proj_pe1"] = (sPE[1].h, sPE[1].n)

    chunk_tensor_done = [None] * NCH

    def scan():
        b.wait("vector", E["proj_last_mm"])
        b.op("vector", lambda q: q.memset(t_Sf[:], 0.0))
        ev_s0 = b.op("vector", lambda q: q.memset(t_Sb[:], 0.0), sem=sV)
        at_evac_ev = [None, None]
        out_free_ev = None
        su_free_ev = None
        out_mm_prev = None
        xdma_ev = [None] * NCH
        x2dma_ev = [None] * NCH
        chunk_done_v = [None] * NCH
        suev = None
        su_mm_ev = None
        for c in range(NCH):
            par = c % 2
            col = c * CH
            if c < 2:
                b.wait("sync", E["proj_pe0"])
                b.wait("sync", E["proj_pe1"])
            else:
                b.wait("sync", chunk_tensor_done[c - 2])
            ev_qc = b.op("sync", lambda q, _par=par, _col=col: q.dma_start(
                out=t_qc[_par][:].rearrange("p (k t) -> p k t", k=16),
                in_=d_sc["q"][:, _col:_col + CH].rearrange("(k p) t -> p k t", p=128)),
                sem=sQC[par], inc=16)
            ev_kc = b.op("sync", lambda q, _par=par, _col=col: q.dma_start(
                out=t_kc[_par][:].rearrange("p (k t) -> p k t", k=16),
                in_=d_sc["k"][:, _col:_col + CH].rearrange("(k p) t -> p k t", p=128)),
                sem=sKC[par], inc=16)
            ev_kn = ev_vn = None
            for jb in (0, 1):
                ev_kn = b.op("sync", lambda q, _par=par, _col=col, _jb=jb: q.dma_start(
                    out=kn3[_par][:, _jb, :],
                    in_=d_sc["k"][:, _col + _jb * 128:_col + (_jb + 1) * 128],
                    transpose=True), sem=sKN[par], inc=16)
                ev_vn = b.op("sync", lambda q, _par=par, _col=col, _jb=jb: q.dma_start(
                    out=vn3[_par][:, _jb, :],
                    in_=d_sc["v"][:, _col + _jb * 128:_col + (_jb + 1) * 128],
                    transpose=True), sem=sVN[par], inc=16)
            if c >= 1:
                b.wait("scalar", chunk_done_v[c - 1])
            xdma_ev[c] = b.op("scalar", lambda q, _col=col: q.dma_start(
                out=xc3, in_=d_xp[:, _col:_col + CH]
                .rearrange("(k p) t -> p k t", p=128)), sem=sXC, inc=16)
            b.wait("vector", ev_kn)
            if c == 0:
                b.wait("vector", E["attn_consts"])
                b.wait("vector", ev_s0)
            for jb in (0, 1):
                b.op("vector", lambda q, _par=par, _jb=jb: q.tensor_tensor(
                    kn3[_par][:, _jb, :], kn3[_par][:, _jb, :], kst3[:, _jb, :], ALU.mult),
                    sem=sV)
            for h in range(HEADS):
                hp = h % 2
                ab0 = 0 if hp == 0 else 2
                b.wait("vector", out_mm_prev)
                if h == 0:
                    b.wait("vector", ev_qc)
                qev = None
                for kk in (0, 1):
                    qev = b.op("vector", lambda q, _par=par, _h=h, _kk=kk: q.tensor_tensor(
                        qsh2[:, _kk, :], qc3[_par][:, 2 * _h + _kk, :],
                        t_cs[:, _h * CH:(_h + 1) * CH], ALU.mult), sem=sV)
                b.wait("tensor", qev)
                if h == 0:
                    b.wait("tensor", ev_kc)
                b.wait("tensor", at_evac_ev[hp])
                ev = None
                for jb in (0, 1):
                    for kk in (0, 1):
                        ev = b.op("tensor", lambda q, _par=par, _h=h, _jb=jb, _kk=kk,
                                  _b0=ab0: q.matmul(
                            pbank(_b0 + _jb, 256),
                            kc3[_par][:, 2 * _h + _kk, _jb * 128:(_jb + 1) * 128],
                            qc3[_par][:, 2 * _h + _kk, :],
                            start=(_kk == 0), stop=(_kk == 1)),
                            sem=sT if (jb == 1 and kk == 1) else None)
                b.wait("vector", ev)
                aev = None
                for jb in (0, 1):
                    aev = b.op("vector", lambda q, _jb=jb, _h=h, _b0=ab0: q.tensor_tensor(
                        at2[:, _jb, :], pbank(_b0 + _jb, 256),
                        dt3[:, _jb, _h * CH:(_h + 1) * CH], ALU.mult), sem=sV)
                at_evac_ev[hp] = aev
                b.wait("tensor", aev)
                b.wait("tensor", out_free_ev)
                if h == 0:
                    b.wait("tensor", ev_vn)
                    if c >= 1:
                        b.wait("tensor", chunk_done_v[c - 1])
                ev = None
                for eb in (0, 1):
                    for jb in (0, 1):
                        b.op("tensor", lambda q, _par=par, _h=h, _eb=eb, _jb=jb: q.matmul(
                            pbank(4 + _eb, 256),
                            vn3[_par][:, _jb, _h * DH + _eb * 128:_h * DH + (_eb + 1) * 128],
                            at2[:, _jb, :], start=(_jb == 0), stop=False))
                    for kk in (0, 1):
                        ev = b.op("tensor", lambda q, _h=h, _eb=eb, _kk=kk: q.matmul(
                            pbank(4 + _eb, 256),
                            Sb3[:, _kk, _h, _eb * 128:(_eb + 1) * 128],
                            qsh2[:, _kk, :], start=False, stop=(_kk == 1)),
                            sem=sT if (eb == 1 and kk == 1) else None)
                out_mm_ev = ev
                out_mm_prev = ev
                b.wait("tensor", su_free_ev)
                ev = None
                for kk in (0, 1):
                    for jb in (0, 1):
                        ev = b.op("tensor", lambda q, _par=par, _h=h, _kk=kk, _jb=jb:
                                  q.matmul(
                            pbank(6 + _kk, 256),
                            kn3[_par][:, _jb, _h * DH + _kk * 128:_h * DH + (_kk + 1) * 128],
                            vn3[_par][:, _jb, _h * DH:(_h + 1) * DH],
                            start=(_jb == 0), stop=(_jb == 1)),
                            sem=sT if (kk == 1 and jb == 1) else None)
                su_mm_ev = ev
                b.wait("vector", out_mm_ev)
                b.wait("vector", xdma_ev[c])
                if c >= 1 and h == 0:
                    b.wait("vector", x2dma_ev[c - 1])
                oev = None
                for eb in (0, 1):
                    oev = b.op("vector", lambda q, _h=h, _eb=eb: q.tensor_tensor(
                        x23[:, 2 * _h + _eb, :], pbank(4 + _eb, 256),
                        xc3[:, 2 * _h + _eb, :], ALU.add), sem=sV)
                out_free_ev = oev
                b.wait("vector", su_mm_ev)
                for kk in (0, 1):
                    b.op("vector", lambda q, _h=h, _kk=kk: q.scalar_tensor_tensor(
                        Sf3[:, _kk, _h, :], Sf3[:, _kk, _h, :], CD[_h],
                        pbank(6 + _kk, 256), ALU.mult, ALU.add))
                    suev = b.op("vector", lambda q, _h=h, _kk=kk: q.tensor_copy(
                        Sb3[:, _kk, _h, :], Sf3[:, _kk, _h, :]), sem=sV)
                su_free_ev = suev
            chunk_done_v[c] = suev
            chunk_tensor_done[c] = su_mm_ev
            b.wait("scalar", out_free_ev)
            x2dma_ev[c] = b.op("scalar", lambda q, _col=col: q.dma_start(
                out=d_xpp[:, _col:_col + CH].rearrange("(k p) t -> p k t", p=128),
                in_=x23), sem=sX2, inc=16)
        b.wait("gpsimd", chunk_done_v[NCH - 1])
        ev_ai = b.op("gpsimd", lambda q: q.dma_start(
            out=d_agin.rearrange("(k p) e -> p k e", p=128),
            in_=t_Sf[:].rearrange("p (k e) -> p k e", k=16)), sem=sCC, inc=16)
        b.wait("gpsimd", ev_ai)
        E["ag_done"] = b.op("gpsimd", lambda q: q.collective_compute(
            "AllGather", ALU.bypass,
            replica_groups=[[0, 1], [2, 3], [4, 5], [6, 7]],
            ins=[d_agin.opt()], outs=[d_agout.opt()]), sem=sCC, inc=1)
        E["scan_x2_last"] = x2dma_ev[NCH - 1]

    corr_t_done = [None] * NCH

    def corr():
        b.wait("gpsimd", E["ag_done"])
        ev_b0 = b.op("gpsimd", lambda q: q.dma_start(
            out=t_T[:].rearrange("p (k e) -> p k e", k=16),
            in_=d_agout[0:DM, :].rearrange("(k p) e -> p k e", p=128)), sem=sCC, inc=16)
        b.wait("gpsimd", E["scan_x2_last"])
        b.wait("gpsimd", ev_b0)
        ev_b1 = b.op("gpsimd", lambda q: q.dma_start(
            out=t_x2[:].rearrange("p (k e) -> p k e", k=16),
            in_=d_agout[DM:2 * DM, :].rearrange("(k p) e -> p k e", p=128)), sem=sCC, inc=16)
        b.wait("vector", ev_b1)
        b.op("vector", lambda q: q.tensor_scalar_mul(t_T[:], t_T[:], t_msk[:, 0:1]))
        ev_T = b.op("vector", lambda q: q.scalar_tensor_tensor(
            t_T[:], t_x2[:], t_msk[:, 1:2], t_T[:], ALU.mult, ALU.add), sem=sV)
        prev_v = ev_T
        prev_xdma = None
        for c in range(NCH):
            par = c % 2
            col = c * CH
            b.wait("vector", prev_v)
            ev_tb = b.op("vector", lambda q: q.tensor_copy(t_Tb[:], t_T[:]), sem=sV)
            if c >= 2:
                b.wait("sync", corr_t_done[c - 2])
            else:
                b.wait("sync", E["scan_x2_last"])
            ev_q = b.op("sync", lambda q, _par=par, _col=col: q.dma_start(
                out=t_qc[_par][:].rearrange("p (k t) -> p k t", k=16),
                in_=d_sc["q"][:, _col:_col + CH].rearrange("(k p) t -> p k t", p=128)),
                sem=sQC[par], inc=16)
            if c == 0:
                b.wait("scalar", E["scan_x2_last"])
            if prev_xdma is not None:
                b.wait("scalar", prev_xdma)
            ev_x = b.op("scalar", lambda q, _col=col: q.dma_start(
                out=xc3, in_=d_xpp[:, _col:_col + CH]
                .rearrange("(k p) t -> p k t", p=128)), sem=sXC, inc=16)
            last_t = None
            oev = None
            for h in range(HEADS):
                if h == 0:
                    b.wait("vector", ev_q)
                b.wait("vector", last_t)
                qev = None
                for kk in (0, 1):
                    qev = b.op("vector", lambda q, _par=par, _h=h, _kk=kk: q.tensor_tensor(
                        qsh2[:, _kk, :], qc3[_par][:, 2 * _h + _kk, :],
                        t_cs[:, _h * CH:(_h + 1) * CH], ALU.mult), sem=sV)
                b.wait("tensor", qev)
                if h == 0:
                    b.wait("tensor", ev_tb)
                if oev is not None:
                    b.wait("tensor", oev)
                for eb in (0, 1):
                    for kk in (0, 1):
                        last_t = b.op("tensor", lambda q, _h=h, _eb=eb, _kk=kk: q.matmul(
                            pbank(_eb, 256),
                            Tb3[:, _kk, _h, _eb * 128:(_eb + 1) * 128],
                            qsh2[:, _kk, :], start=(_kk == 0), stop=(_kk == 1)),
                            sem=sT if (eb == 1 and kk == 1) else None)
                b.wait("vector", last_t)
                if h == 0:
                    b.wait("vector", ev_x)
                for eb in (0, 1):
                    oev = b.op("vector", lambda q, _h=h, _eb=eb: q.tensor_tensor(
                        xc3[:, 2 * _h + _eb, :], pbank(_eb, 256),
                        xc3[:, 2 * _h + _eb, :], ALU.add), sem=sV)
            corr_t_done[c] = last_t
            b.wait("vector", last_t)
            tev = None
            for kk in (0, 1):
                for h in range(HEADS):
                    tev = b.op("vector", lambda q, _kk=kk, _h=h: q.tensor_scalar_mul(
                        T3[:, _kk, _h, :], T3[:, _kk, _h, :], CD[_h]), sem=sV)
            prev_v = tev
            b.wait("scalar", oev)
            prev_xdma = b.op("scalar", lambda q, _col=col: q.dma_start(
                out=d_xpp[:, _col:_col + CH].rearrange("(k p) t -> p k t", p=128),
                in_=xc3), sem=sXC, inc=16)
        E["corr_last_xdma"] = prev_xdma

    if TESTMODE:
        if TESTMODE == 1:
            ph_init()
        else:
            b.op("gpsimd", lambda q: q.dma_start(out=t_msk[:], in_=d_msk), sem=sInit, inc=16)
            E["init"] = b.op("gpsimd", lambda q: q.dma_start(
                out=t_bias["v"][:], in_=d_b["v"]), sem=sInit, inc=16)
            b.op("vector", lambda q: q.memset(t_ones[:], 1.0))
            b.op("vector", lambda q: q.memset(t_onesf[:], 1.0))
            b.op("vector", lambda q: q.memset(t_hm[:], -0.5))
            b.op("vector", lambda q: q.memset(t_zero[:], 0.0))
            E["zeros"] = b.op("vector", lambda q: q.memset(t_hp[:], 0.5), sem=sV)
            E["s_ready"] = E["zeros"]
        ev_ln = ln_block(d_x, 0, 0.5, ln3, 0)
        b.wait("gpsimd", ev_ln)
        ev_f = b.op("gpsimd", lambda q: q.dma_start(out=d_dbgl[0], in_=t_ln[:]),
                    sem=sDBG, inc=16)
        b.wait("gpsimd", ev_f)
        b.wait("scalar", ev_f)
        b.wait("sync", ev_f)
        return b.finish()

    ph_init()

    ev_ln = ln_block(d_x, 0, 0.5, ln3, 0)
    if DEBUG:
        b.wait("gpsimd", ev_ln)
        b.op("gpsimd", lambda q: q.dma_start(out=d_dbgl[0], in_=t_ln[:]), sem=sDBG, inc=16)
    ev_hid = x1x2("quant", ev_ln)
    if DEBUG:
        b.wait("gpsimd", ev_hid)
        b.op("gpsimd", lambda q: q.dma_start(out=d_dbgh[0], in_=t_hid[:]), sem=sDBG, inc=16)
    w3a = w3_stage("quant", 0, d_x, d_xp, ev_hid)
    ev_ln = ln_block(d_x, 1, 0.5, ln3, 0, wait_load=[("scalar", w3a[2])])
    sc_gates = [(sWSQ[i].h, sWSQ[i].n) for i in range(4)]
    ev_hid = x1x2("scratch", ev_ln, wait_w=[w3a[2]], gate_reads=sc_gates)
    if DEBUG:
        b.wait("gpsimd", ev_hid)
        b.op("gpsimd", lambda q: q.dma_start(out=d_dbgh[1], in_=t_hid[:]), sem=sDBG, inc=16)
        b.op("gpsimd", lambda q: q.dma_start(out=d_dbgl[1], in_=t_ln[:]), sem=sDBG, inc=16)
    w3_gates = [(sW3SQ[i].h, sW3SQ[i].n) for i in range(2)]
    w3b = w3_stage("scratch", 1, d_x, d_xp, ev_hid, gate_reads=w3_gates)

    ev_l2a = ln_block(d_xp, 0, 1.0, ln23, 0,
                      wait_load=[("scalar", w3b[2]), ("scalar", (sXO.h, sXO.n))])
    ev_l2b = ln_block(d_xp, 1, 1.0, ln23, HALF, wait_load=[("scalar", ev_l2a)])

    b.wait("gpsimd", ev_l2b)
    b.op("gpsimd", lambda q: q.dma_start(
        out=dt3, in_=d_dt.rearrange("(j p) n -> p j n", p=128)), sem=sTabs, inc=16)
    b.op("gpsimd", lambda q: q.dma_start(out=t_cs[:], in_=d_cs), sem=sTabs, inc=16)
    E["attn_consts"] = b.op("gpsimd", lambda q: q.dma_start(
        out=kst3, in_=d_kst.rearrange("(j p) n -> p j n", p=128)), sem=sTabs, inc=16)

    proj(ev_l2b)
    scan()
    corr()

    b.barrier(sBar)

    ev_ln = ln_block(d_xpp, 0, 0.5, ln3, 0,
                     wait_load=[("scalar", E["corr_last_xdma"])])
    ev_hid = x1x2("scratch", ev_ln, gate_reads=sc_gates)
    w3f2 = w3_stage("scratch", 0, d_xpp, d_out, ev_hid, gate_reads=w3_gates)
    ev_ln = ln_block(d_xpp, 1, 0.5, ln3, 0, wait_load=[("scalar", w3f2[2])])
    ev_hid = x1x2("scratch", ev_ln, wait_w=[w3f2[2]])
    w3g = w3_stage("scratch", 1, d_xpp, d_out, ev_hid)

    b.wait("scalar", (sO.h, sO.n))
    b.wait("sync", (sO.h, sO.n))

    return b.finish()


def _host_tables():
    idx = np.arange(CH)
    dt = np.zeros((CH, HEADS * CH), np.float32)
    cs = np.zeros((128, HEADS * CH), np.float32)
    kst = np.zeros((CH, DM), np.float32)
    for h in range(HEADS):
        g = _g[h]
        D = np.where(idx[:, None] >= idx[None, :],
                     g ** (idx[:, None] - idx[None, :]).astype(np.float64), 0.0)
        dt[:, h * CH:(h + 1) * CH] = D.T.astype(np.float32)
        cs[:, h * CH:(h + 1) * CH] = np.broadcast_to(
            (g ** (idx + 1).astype(np.float64)).astype(np.float32), (128, CH))
        kst[:, h * DH:(h + 1) * DH] = np.broadcast_to(
            (g ** (CH - 1 - idx).astype(np.float64)).astype(np.float32)[:, None],
            (CH, DH))
    return dt, cs, kst


_CACHE = {}


def kernel(L0_attn_x, L0_attn_Wq, L0_attn_bq, L0_attn_Wk, L0_attn_bk,
           L0_attn_Wv, L0_attn_bv, L0_ffn_w1, L0_ffn_w2, L0_ffn_w3):
    x = np.ascontiguousarray(np.asarray(L0_attn_x, np.float32))
    Wq = np.ascontiguousarray(np.asarray(L0_attn_Wq, np.float32))
    Wk = np.ascontiguousarray(np.asarray(L0_attn_Wk, np.float32))
    Wv = np.ascontiguousarray(np.asarray(L0_attn_Wv, np.float32))
    bq = np.asarray(L0_attn_bq, np.float32)
    bk = np.asarray(L0_attn_bk, np.float32)
    bv = np.asarray(L0_attn_bv, np.float32)
    w1 = np.asarray(L0_ffn_w1, np.float32)
    w2 = np.asarray(L0_ffn_w2, np.float32)
    w3 = np.asarray(L0_ffn_w3, np.float32)

    w1T = np.ascontiguousarray(w1.T)
    w2T = np.ascontiguousarray(w2.T)
    w3T = np.ascontiguousarray(w3.T)

    dt, cs, kst = _host_tables()
    bq_r = np.ascontiguousarray(bq.reshape(16, 128).T)
    bk_r = np.ascontiguousarray(bk.reshape(16, 128).T)
    bv_r = np.ascontiguousarray(bv.reshape(16, 128).T)

    if "nc" not in _CACHE:
        _CACHE["nc"] = build_program()
    nc = _CACHE["nc"]

    in_maps = []
    for c in range(8):
        bb, hh = divmod(c, 2)
        xT = np.ascontiguousarray(x[bb, hh * TOK:(hh + 1) * TOK, :].T)
        wsh = np.empty((3, 128, 11264), np.float32)
        wsh[0] = w1T[256 * c:256 * (c + 1), :].reshape(2, 128, FH) \
            .transpose(1, 0, 2).reshape(128, 2 * FH)
        wsh[1] = w2T[256 * c:256 * (c + 1), :].reshape(2, 128, FH) \
            .transpose(1, 0, 2).reshape(128, 2 * FH)
        wsh[2] = w3T[:, 256 * c:256 * (c + 1)].reshape(NFB, 128, 256) \
            .transpose(1, 0, 2).reshape(128, NFB * 256)
        cmask = np.zeros((128, 2), np.float32)
        if hh == 1:
            cmask[:, 0] = 1.0
        in_maps.append({
            "xT": xT, "w1T": w1T, "w2T": w2T, "w3T": w3T,
            "Wq": Wq, "Wk": Wk, "Wv": Wv, "wsh": wsh,
            "bq_r": bq_r, "bk_r": bk_r, "bv_r": bv_r,
            "dt_tab": dt, "cs_tab": cs, "ks_tab": kst, "cmask": cmask,
        })

    res = run_bass_kernel_spmd(nc, in_maps, core_ids=list(range(8)))
    _CACHE["res"] = res
    out = np.empty((4, 4096, DM), np.float32)
    for c in range(8):
        bb, hh = divmod(c, 2)
        out[bb, hh * TOK:(hh + 1) * TOK, :] = res.results[c]["outT"].T
    return out


# version 03
# speedup vs baseline: 1.0209x; 1.0209x over previous
import numpy as np
import ml_dtypes

import concourse.bass as bass
import concourse.mybir as mybir
from concourse.bass_utils import run_bass_kernel_spmd

F32 = mybir.dt.float32
BF16 = mybir.dt.bfloat16
AF = mybir.ActivationFunctionType
ALU = mybir.AluOpType
AX = mybir.AxisListType.X
BF = ml_dtypes.bfloat16

DM = 2048
TOK = 2048
HALF = 1024
FH = 5632
NFB = FH // 128
NDT = DM // 128
HEADS = 8
DH = 256
CH = 256
NCH = TOK // CH
WCNT = float(FH * DM)
EPSQ = 1e-8
EPSLN = 1e-5
KB = 1024

ENGS = ("tensor", "vector", "scalar", "gpsimd", "sync")
DEBUG = False
TESTMODE = 0

_betas = np.linspace(np.log2(1.0 / 32.0), np.log2(1.0 / 512.0), HEADS)
_g = 1.0 - np.exp2(_betas)
CD = [float(v) for v in (_g ** CH)]


class _Sem:
    __slots__ = ("h", "n")

    def __init__(self, h):
        self.h = h
        self.n = 0


class B:
    def __init__(self):
        self.nc = bass.Bass("TRN2", target_bir_lowering=False)
        self.prog = {e: [] for e in ENGS}
        self._ctxs = []

    def sem(self, name):
        ctx = self.nc.semaphore(name)
        h = ctx.__enter__()
        self._ctxs.append(ctx)
        return _Sem(h)

    def op(self, eng, build, sem=None, inc=1):
        if sem is None:
            self.prog[eng].append(build)
            return None

        def f(q, _b=build, _s=sem.h, _n=inc):
            _b(q).then_inc(_s, _n)

        self.prog[eng].append(f)
        sem.n += inc
        return (sem.h, sem.n)

    def wait(self, eng, ev):
        if ev is None:
            return
        h, v = ev
        self.prog[eng].append(lambda q, _h=h, _v=v: q.wait_ge(_h, _v))

    def chain(self, eng, sem, builds, first_ev=None):
        ev = first_ev
        for f in builds:
            if ev is not None:
                self.wait(eng, ev)
            ev = self.op(eng, f, sem=sem)
        return ev

    def barrier(self, bsem):
        for e in ENGS:
            self.op(e, lambda q: q.nop(), sem=bsem, inc=1)
        tgt = bsem.n
        for e in ENGS:
            self.wait(e, (bsem.h, tgt))

    def finish(self):
        nc = self.nc
        with nc.Block() as block:
            @block.tensor
            def _(q):
                for f in self.prog["tensor"]:
                    f(q)

            @block.vector
            def _(q):
                for f in self.prog["vector"]:
                    f(q)

            @block.scalar
            def _(q):
                for f in self.prog["scalar"]:
                    f(q)

            @block.gpsimd
            def _(q):
                for f in self.prog["gpsimd"]:
                    f(q)

            @block.sync
            def _(q):
                for f in self.prog["sync"]:
                    f(q)
        for ctx in self._ctxs:
            ctx.__exit__(None, None, None)
        return nc


def build_program():
    b = B()
    nc = b.nc

    d_x = nc.dram_tensor("xT", [DM, TOK], F32, kind="ExternalInput").ap()
    d_w1T = nc.dram_tensor("w1T", [DM, FH], F32, kind="ExternalInput").ap()
    d_w2T = nc.dram_tensor("w2T", [DM, FH], F32, kind="ExternalInput").ap()
    d_w3T = nc.dram_tensor("w3T", [FH, DM], F32, kind="ExternalInput").ap()
    d_W = {
        "q": nc.dram_tensor("Wq", [DM, DM], F32, kind="ExternalInput").ap(),
        "k": nc.dram_tensor("Wk", [DM, DM], F32, kind="ExternalInput").ap(),
        "v": nc.dram_tensor("Wv", [DM, DM], F32, kind="ExternalInput").ap(),
    }
    d_wsh = nc.dram_tensor("wsh", [3, 128, 11264], F32, kind="ExternalInput").ap()
    d_b = {
        "q": nc.dram_tensor("bq_r", [128, 16], F32, kind="ExternalInput").ap(),
        "k": nc.dram_tensor("bk_r", [128, 16], F32, kind="ExternalInput").ap(),
        "v": nc.dram_tensor("bv_r", [128, 16], F32, kind="ExternalInput").ap(),
    }
    d_dt = nc.dram_tensor("dt_tab", [CH, HEADS * CH], F32, kind="ExternalInput").ap()
    d_cs = nc.dram_tensor("cs_tab", [128, HEADS * CH], F32, kind="ExternalInput").ap()
    d_kst = nc.dram_tensor("ks_tab", [CH, DM], F32, kind="ExternalInput").ap()
    d_msk = nc.dram_tensor("cmask", [128, 2], F32, kind="ExternalInput").ap()

    d_out = nc.dram_tensor("outT", [DM, TOK], F32, kind="ExternalOutput").ap()
    d_xp = nc.dram_tensor("xp", [DM, TOK], F32, kind="ExternalOutput").ap()
    d_xpp = nc.dram_tensor("xpp", [DM, TOK], F32, kind="ExternalOutput").ap()
    _dk = "ExternalOutput" if DEBUG else "Internal"
    d_w1q = nc.dram_tensor("w1q_s", [NFB, 128, DM], BF16, kind=_dk).ap()
    d_w2q = nc.dram_tensor("w2q_s", [NFB, 128, DM], BF16, kind=_dk).ap()
    d_w3q = nc.dram_tensor("w3q_s", [NDT, 128, FH], BF16, kind=_dk).ap()
    d_sc = {
        "q": nc.dram_tensor("qT_s", [DM, TOK], BF16, kind="Internal").ap(),
        "k": nc.dram_tensor("kT_s", [DM, TOK], BF16, kind="Internal").ap(),
        "v": nc.dram_tensor("vT_s", [DM, TOK], BF16, kind="Internal").ap(),
    }
    d_ab = nc.dram_tensor("ab_s", [2, HALF], F32, kind="Internal").ap()
    d_gs = nc.dram_tensor("g_s", [1, 8], F32, kind="Internal").ap()
    d_gin = nc.dram_tensor("g_in", [128, 128], F32, kind="Internal").ap()
    d_gout = nc.dram_tensor("g_out", [128, 128], F32, kind="Internal").ap()
    d_agin = nc.dram_tensor("ag_in", [DM, DH], F32, kind="Internal").ap()
    d_agout = nc.dram_tensor("ag_out", [2 * DM, DH], F32, kind="Internal").ap()
    if DEBUG:
        d_dbg = nc.dram_tensor("dbg", [128, 64], F32, kind="ExternalOutput").ap()
        d_dbgb = nc.dram_tensor("dbgb", [128, DM], BF16, kind="ExternalOutput").ap()
        d_dbgh = nc.dram_tensor("dbgh", [2, 128, NFB * HALF], BF16,
                                kind="ExternalOutput").ap()
        d_dbga = nc.dram_tensor("dbga", [6, HALF], F32, kind="ExternalOutput").ap()
        d_dbgr = nc.dram_tensor("dbgr", [12, HALF], F32, kind="ExternalOutput").ap()
        d_dbgl = nc.dram_tensor("dbgl", [2, 128, NDT * HALF], BF16,
                                kind="ExternalOutput").ap()

    SB0 = 16 * KB

    def A(name, shape, dtype, offset):
        return nc.alloc_sbuf_tensor_at(name, shape, dtype, offset=SB0 + offset)

    t_ones = A("ones_bf", [128, 2], BF16, offset=0)
    t_onesf = A("ones_f", [128, 2], F32, offset=32)
    t_sb = A("s_b", [128, 8], F32, offset=64)
    t_msk = A("cmask_t", [128, 2], F32, offset=96)
    t_gacc = A("gacc", [128, 4], F32, offset=128)
    t_hm = A("half_m", [128, 1], F32, offset=160)
    t_hp = A("half_p", [128, 1], F32, offset=192)
    t_zero = A("zero_t", [128, 1], F32, offset=208 if False else 13 * KB)
    t_bias = {
        "q": A("bq_t", [128, 16], F32, offset=224),
        "k": A("bk_t", [128, 16], F32, offset=288),
        "v": A("bv_t", [128, 16], F32, offset=352),
    }
    t_gbuf = A("gbuf", [128, 128], F32, offset=416)
    t_grow = A("g_row", [1, 8], F32, offset=928)
    t_arow = A("a_row", [1, HALF], F32, offset=960)
    t_brow = A("b_row", [1, HALF], F32, offset=960 + 4 * HALF)
    t_trow = A("t_row", [1, HALF], F32, offset=960 + 8 * HALF)

    t_ab = A("ab_b", [128, 2 * HALF], F32, offset=14 * KB)
    t_dt = A("dt_tab_t", [128, 2 * HEADS * CH], F32, offset=14 * KB)
    t_cs = A("cs_tab_t", [128, HEADS * CH], F32, offset=30 * KB)
    t_kst = A("ks_tab_t", [128, 2 * DM], F32, offset=38 * KB)
    t_wf = A("w_f32", [128, 4 * 1024], F32, offset=22 * KB)
    t_wq = A("w_q", [128, 4 * DM], BF16, offset=38 * KB)
    t_wqb = A("w_qb", [128, 4 * DM], BF16, offset=22 * KB)
    t_w3f = A("w3_f32", [128, 22 * 128], F32, offset=22 * KB)
    t_w3q = A("w3_q", [128, 2 * NFB * 128], BF16, offset=33 * KB)
    t_wc = A("w_cast", [128, 2 * NDT * 128], BF16, offset=55 * KB)
    t_w3tmp = A("w3_tmp", [128, 22 * 128], BF16, offset=55 * KB)

    OFF_R1 = 63 * KB
    t_xa = A("xA", [128, NDT * HALF], F32, offset=OFF_R1)
    t_ln = A("lnTb", [128, NDT * HALF], BF16, offset=OFF_R1)
    t_Sf = A("S_f32", [128, 2 * HEADS * DH], F32, offset=OFF_R1)
    t_Sb = A("S_bf", [128, 2 * HEADS * DH], BF16, offset=OFF_R1 + 16 * KB)
    t_T = A("T_f32", [128, NDT * CH], F32, offset=OFF_R1)
    t_Tb = A("T_bf", [128, NDT * CH], BF16, offset=OFF_R1 + 16 * KB)
    t_xc = A("x_chunk", [128, NDT * CH], F32, offset=OFF_R1 + 24 * KB)
    t_x2 = A("x2_chunk", [128, NDT * CH], F32, offset=OFF_R1 + 40 * KB)

    t_hid = A("hidden", [128, NFB * HALF], BF16, offset=95 * KB)

    OFF_R2 = 127 * KB
    t_ln2 = A("ln2Tb", [128, NDT * TOK], BF16, offset=OFF_R2)
    t_gsh = A("g_shard", [128, 2 * 5632], F32, offset=OFF_R2)
    t_qc = [A("qc0", [128, NDT * CH], BF16, offset=OFF_R2),
            A("qc1", [128, NDT * CH], BF16, offset=OFF_R2 + 32 * KB)]
    t_kc = [A("kc0", [128, NDT * CH], BF16, offset=OFF_R2 + 8 * KB),
            A("kc1", [128, NDT * CH], BF16, offset=OFF_R2 + 40 * KB)]
    t_kn = [A("kn0", [128, 2 * DM], BF16, offset=OFF_R2 + 16 * KB),
            A("kn1", [128, 2 * DM], BF16, offset=OFF_R2 + 48 * KB)]
    t_vn = [A("vn0", [128, 2 * DM], BF16, offset=OFF_R2 + 24 * KB),
            A("vn1", [128, 2 * DM], BF16, offset=OFF_R2 + 56 * KB)]

    t_tmp1 = A("tmp1", [128, HALF], F32, offset=191 * KB)
    t_pe0 = A("pe0", [128, TOK], BF16, offset=191 * KB)
    t_xout = A("xout", [128, HALF], F32, offset=195 * KB)
    t_pe1 = A("pe1", [128, TOK], BF16, offset=195 * KB)
    t_stbf = A("st_bf", [128, 4 * HALF], BF16, offset=199 * KB)
    t_qsh = A("qs_h", [128, 2 * CH], BF16, offset=OFF_R1 + 56 * KB)
    t_at = A("attnS", [128, 2 * CH], BF16, offset=OFF_R1 + 57 * KB)

    ps = nc.alloc_psum_tensor("ps", [128, 4096], F32)
    psA = ps.ap()

    def pbank(i, w=512):
        return psA[:, 512 * i:512 * i + w]

    def v3(t, k, n):
        return t.ap().rearrange("p (k n) -> p k n", k=k, n=n)

    ln3 = v3(t_ln, NDT, HALF)
    ln23 = v3(t_ln2, NDT, TOK)
    xa3 = v3(t_xa, NDT, HALF)
    hid3 = v3(t_hid, NFB, HALF)
    wf3 = t_wf.ap().rearrange("p (m k f) -> p m k f", m=2, k=NDT, f=128)
    wq3 = t_wq.ap().rearrange("p (pp m n) -> p pp m n", pp=2, m=2, n=DM)
    wqb3 = t_wqb.ap().rearrange("p (pp m n) -> p pp m n", pp=2, m=2, n=DM)
    w3q3 = t_w3q.ap().rearrange("p (pp n) -> p pp n", pp=2, n=NFB * 128)
    wc3 = t_wc.ap().rearrange("p (pp n) -> p pp n", pp=2, n=NDT * 128)
    ab2 = v3(t_ab, 2, HALF)
    stbf2 = v3(t_stbf, 4, HALF)
    gsh2 = v3(t_gsh, 2, 5632)
    dt3 = v3(t_dt, 2, HEADS * CH)
    kst3 = v3(t_kst, 2, DM)
    Sf3 = t_Sf.ap().rearrange("p (k h e) -> p k h e", k=2, h=HEADS, e=DH)
    Sb3 = t_Sb.ap().rearrange("p (k h e) -> p k h e", k=2, h=HEADS, e=DH)
    T3 = t_T.ap().rearrange("p (k h e) -> p k h e", k=2, h=HEADS, e=DH)
    Tb3 = t_Tb.ap().rearrange("p (k h e) -> p k h e", k=2, h=HEADS, e=DH)
    xc3 = v3(t_xc, NDT, CH)
    x23 = v3(t_x2, NDT, CH)
    at2 = v3(t_at, 2, CH)
    qsh2 = v3(t_qsh, 2, CH)
    qc3 = [v3(t, NDT, CH) for t in t_qc]
    kc3 = [v3(t, NDT, CH) for t in t_kc]
    kn3 = [v3(t, 2, DM) for t in t_kn]
    vn3 = [v3(t, 2, DM) for t in t_vn]
    lnq = t_ln.ap()

    sT = b.sem("sT")
    sV = b.sem("sV")
    sA = b.sem("sA")
    sBar = b.sem("sBar")
    sO = b.sem("sO")
    sInit = b.sem("sInit")
    sTabs = b.sem("sTabs")
    sGM = b.sem("sGM")
    sCC = b.sem("sCC")
    sGSH = [b.sem(f"sGSH{i}") for i in range(2)]
    sWF = [b.sem(f"sWF{i}") for i in range(4)]
    sWSQ = [b.sem(f"sWSQ{i}") for i in range(4)]
    sWQB = [b.sem(f"sWQB{i}") for i in range(4)]
    sW3F = b.sem("sW3F")
    sW3SQ = [b.sem(f"sW3SQ{i}") for i in range(2)]
    sW3R = [b.sem(f"sW3R{i}") for i in range(2)]
    sXA = b.sem("sXA")
    sAB = b.sem("sAB")
    sXIN = b.sem("sXIN")
    sXO = b.sem("sXO")
    sWC = [b.sem(f"sWC{i}") for i in range(2)]
    sPE = [b.sem(f"sPE{i}") for i in range(2)]
    sQC = [b.sem(f"sQC{i}") for i in range(2)]
    sKC = [b.sem(f"sKC{i}") for i in range(2)]
    sKN = [b.sem(f"sKN{i}") for i in range(2)]
    sVN = [b.sem(f"sVN{i}") for i in range(2)]
    sXC = b.sem("sXC")
    sX2 = b.sem("sX2")
    if DEBUG:
        sDBG = b.sem("sDBG")

    E = {}

    def ph_init():
        b.op("gpsimd", lambda q: q.dma_start(out=t_msk[:], in_=d_msk), sem=sInit, inc=16)
        b.op("gpsimd", lambda q: q.dma_start(out=t_bias["q"][:], in_=d_b["q"]), sem=sInit, inc=16)
        b.op("gpsimd", lambda q: q.dma_start(out=t_bias["k"][:], in_=d_b["k"]), sem=sInit, inc=16)
        E["init"] = b.op("gpsimd", lambda q: q.dma_start(out=t_bias["v"][:], in_=d_b["v"]),
                         sem=sInit, inc=16)
        b.op("vector", lambda q: q.memset(t_ones[:], 1.0))
        b.op("vector", lambda q: q.memset(t_onesf[:], 1.0))
        b.op("vector", lambda q: q.memset(t_gbuf[:], 0.0))
        b.op("vector", lambda q: q.memset(t_hm[:], -0.5))
        b.op("vector", lambda q: q.memset(t_hp[:], 0.5))
        b.op("vector", lambda q: q.memset(t_zero[:], 0.0))
        E["zeros"] = b.op("vector", lambda q: q.memset(t_gacc[:], 0.0), sem=sV)

        red_evs = []
        for i in range(6):
            m, c = divmod(i, 2)
            if i >= 2:
                b.wait("sync", red_evs[i - 2])
            ev_d = b.op("sync", lambda q, _m=m, _c=c: q.dma_start(
                out=gsh2[:, _c, :], in_=d_wsh[_m, :, _c * 5632:(_c + 1) * 5632]),
                sem=sGSH[c], inc=16)
            if i == 0:
                b.wait("vector", E["zeros"])
            b.wait("vector", ev_d)
            ev_r = b.op("vector", lambda q, _c=c: q.reduce_sum(
                t_gacc[:, 3:4], gsh2[:, _c, :], axis=AX, apply_absolute_value=True), sem=sV)
            b.wait("vector", ev_r)
            red_evs.append(b.op("vector", lambda q, _m=m: q.tensor_tensor(
                t_gacc[:, _m:_m + 1], t_gacc[:, _m:_m + 1], t_gacc[:, 3:4], ALU.add),
                sem=sV))
        b.wait("tensor", red_evs[-1])
        ev_mm = b.op("tensor", lambda q: q.matmul(
            psA[0:1, 0:3], t_onesf[:, 0:1], t_gacc[:, 0:3], start=True, stop=True), sem=sT)
        b.wait("vector", ev_mm)
        ev_gb = b.op("vector", lambda q: q.tensor_copy(t_gbuf[0:1, 0:3], psA[0:1, 0:3]),
                     sem=sV)
        b.wait("gpsimd", ev_gb)
        ev_gi = b.op("gpsimd", lambda q: q.dma_start(out=d_gin, in_=t_gbuf[:]),
                     sem=sGM, inc=16)
        b.wait("gpsimd", ev_gi)
        ev_ar = b.op("gpsimd", lambda q: q.collective_compute(
            "AllReduce", ALU.add, replica_groups=[list(range(8))],
            ins=[d_gin.opt()], outs=[d_gout.opt()]), sem=sGM, inc=1)
        b.wait("gpsimd", ev_ar)
        ev_gr = b.op("gpsimd", lambda q: q.dma_start(out=t_grow[:], in_=d_gout[0:1, 0:8]),
                     sem=sGM, inc=16)
        b.wait("vector", ev_gr)
        ev_s = b.chain("vector", sV, [
            lambda q: q.tensor_scalar(t_grow[0:1, 4:7], t_grow[0:1, 0:3],
                                      1.0 / WCNT, EPSQ, ALU.mult, ALU.add),
            lambda q: q.tensor_scalar_mul(t_grow[0:1, 0:3], t_grow[0:1, 4:7], 0.5),
            lambda q: q.tensor_scalar_mul(t_grow[0:1, 4:7], t_grow[0:1, 0:3], -1.0),
        ])
        b.wait("gpsimd", ev_s)
        ev_so = b.op("gpsimd", lambda q: q.dma_start(out=d_gs[0:1, 0:8], in_=t_grow[0:1, 0:8]),
                     sem=sGM, inc=16)
        b.wait("gpsimd", ev_so)
        E["s_ready"] = b.op("gpsimd", lambda q: q.dma_start(
            out=t_sb[:, 0:8], in_=d_gs[0:1, 0:8].partition_broadcast(128)), sem=sGM, inc=16)
        if DEBUG:
            b.wait("gpsimd", E["s_ready"])
            b.op("gpsimd", lambda q: q.dma_start(out=d_dbg[:, 0:8], in_=t_sb[:]),
                 sem=sDBG, inc=16)
            b.op("gpsimd", lambda q: q.dma_start(out=d_dbg[:, 4:8], in_=t_gacc[:]),
                 sem=sDBG, inc=16)
            b.op("gpsimd", lambda q: q.dma_start(out=d_dbg[0:1, 8:16], in_=t_grow[:]),
                 sem=sDBG, inc=16)
            b.op("gpsimd", lambda q: q.dma_start(out=d_dbg[0:1, 16:24],
                                                 in_=t_gbuf[0:1, 0:8]), sem=sDBG, inc=16)

    def ln_block(x_src, half, scale, dst3, dst_c0, wait_load=()):
        c0 = half * HALF
        for eng, ev in wait_load:
            b.wait(eng, ev)
        ev_x = b.op("scalar", lambda q: q.dma_start(
            out=xa3, in_=x_src[:, c0:c0 + HALF].rearrange("(k p) t -> p k t", p=128)),
            sem=sXA, inc=16)
        mb = psA[0:1, 4 * 512:4 * 512 + HALF]
        qb = psA[0:1, 6 * 512:6 * 512 + HALF]
        b.wait("vector", ev_x)
        b.wait("scalar", ev_x)
        ev_mm = None
        mm_ev = [None] * NDT
        for k in range(NDT):
            sl = 2 * (k % 2)
            if k >= 2:
                b.wait("vector", mm_ev[k - 2])
                b.wait("scalar", mm_ev[k - 2])
            ev_cb = b.op("vector", lambda q, _k=k, _sl=sl: q.tensor_copy(
                stbf2[:, _sl, :], xa3[:, _k, :]), sem=sV)
            ev_sq = b.op("scalar", lambda q, _k=k, _sl=sl: q.activation(
                stbf2[:, _sl + 1, :], xa3[:, _k, :], AF.Square,
                bias=t_zero[:, 0:1], scale=t_onesf[:, 0:1]), sem=sA)
            b.wait("tensor", ev_cb)
            b.wait("tensor", ev_sq)
            for c in (0, 512):
                b.op("tensor", lambda q, _k=k, _c=c, _sl=sl: q.matmul(
                    mb[:, _c:_c + 512], t_ones[:, 0:1], stbf2[:, _sl, _c:_c + 512],
                    start=(_k == 0), stop=(_k == NDT - 1)))
                ev_mm = b.op("tensor", lambda q, _k=k, _c=c, _sl=sl: q.matmul(
                    qb[:, _c:_c + 512], t_ones[:, 0:1], stbf2[:, _sl + 1, _c:_c + 512],
                    start=(_k == 0), stop=(_k == NDT - 1)),
                    sem=sT if c == 512 else None)
            mm_ev[k] = ev_mm
        b.wait("vector", ev_mm)
        if DEBUG and not E.get("_lndbg"):
            ev1 = b.op("vector", lambda q: q.tensor_copy(t_trow[0:1, :], mb), sem=sV)
            b.wait("gpsimd", ev1)
            ev2 = b.op("gpsimd", lambda q: q.dma_start(out=d_dbga[2:3, :],
                                                       in_=t_trow[0:1, :]), sem=sDBG, inc=16)
            b.wait("vector", ev2)
            ev3 = b.op("vector", lambda q: q.tensor_copy(t_trow[0:1, :], qb), sem=sV)
            b.wait("gpsimd", ev3)
            ev4 = b.op("gpsimd", lambda q: q.dma_start(out=d_dbga[3:4, :],
                                                       in_=t_trow[0:1, :]), sem=sDBG, inc=16)
            b.wait("vector", ev4)
        dbg_on = DEBUG and not E.get("_lnrdbg")
        if dbg_on:
            E["_lnrdbg"] = True

        def _dump(row, src_t):
            if not dbg_on:
                return
            ev = b.op("vector", lambda q: q.nop(), sem=sV)
            b.wait("gpsimd", ev)
            evd = b.op("gpsimd", lambda q, _r=row, _s=src_t: q.dma_start(
                out=d_dbgr[_r:_r + 1, :], in_=_s[0:1, :]), sem=sGM, inc=16)
            b.wait("vector", evd)

        steps = [
            (lambda q: q.tensor_scalar_mul(t_arow[0:1, :], mb, 1.0 / DM), 0, t_arow),
            (lambda q: q.tensor_scalar_mul(t_brow[0:1, :], qb, 1.0 / DM), 1, t_brow),
            (lambda q: q.tensor_tensor(t_trow[0:1, :], t_arow[0:1, :], t_arow[0:1, :],
                                       ALU.mult), 2, t_trow),
            (lambda q: q.tensor_tensor(t_brow[0:1, :], t_brow[0:1, :], t_trow[0:1, :],
                                       ALU.subtract), 3, t_brow),
            (lambda q: q.tensor_scalar(t_trow[0:1, :], t_brow[0:1, :], 1.0, EPSLN,
                                       ALU.mult, ALU.add), 4, t_trow),
        ]
        ev1 = None
        for f, row, srct in steps:
            if ev1 is not None:
                b.wait("vector", ev1)
            ev1 = b.op("vector", f, sem=sV)
            _dump(row, srct)
        b.wait("scalar", ev1)
        ev_l = b.op("scalar", lambda q: q.activation(
            t_brow[0:1, :], t_trow[0:1, :], AF.Ln,
            bias=t_zero[0:1, 0:1], scale=t_onesf[0:1, 0:1]), sem=sA)
        b.wait("scalar", ev_l)
        ev_rs = b.op("scalar", lambda q: q.activation(
            t_trow[0:1, :], t_brow[0:1, :], AF.Exp,
            bias=t_zero[0:1, 0:1], scale=t_hm[0:1, 0:1]), sem=sA)
        b.wait("vector", ev_rs)
        _dump(6, t_trow)
        steps2 = [
            (lambda q: q.tensor_scalar_mul(t_trow[0:1, :], t_trow[0:1, :], float(scale)),
             7, t_trow),
            (lambda q: q.tensor_tensor(t_arow[0:1, :], t_arow[0:1, :], t_trow[0:1, :],
                                       ALU.mult), 8, t_arow),
            (lambda q: q.tensor_scalar_mul(t_arow[0:1, :], t_arow[0:1, :], -1.0),
             9, t_arow),
        ]
        ev2 = ev_rs
        for f, row, srct in steps2:
            b.wait("vector", ev2)
            ev2 = b.op("vector", f, sem=sV)
            _dump(row, srct)
        b.wait("scalar", ev2)
        b.op("scalar", lambda q: q.dma_start(out=d_ab[0:1, :], in_=t_trow[0:1, :]),
             sem=sAB, inc=16)
        ev_w = b.op("scalar", lambda q: q.dma_start(out=d_ab[1:2, :], in_=t_arow[0:1, :]),
                    sem=sAB, inc=16)
        b.wait("scalar", ev_w)
        b.op("scalar", lambda q: q.dma_start(
            out=ab2[:, 0, :], in_=d_ab[0:1, :].partition_broadcast(128)), sem=sAB, inc=16)
        ev_bc = b.op("scalar", lambda q: q.dma_start(
            out=ab2[:, 1, :], in_=d_ab[1:2, :].partition_broadcast(128)), sem=sAB, inc=16)
        b.wait("vector", ev_bc)
        if DEBUG and not E.get("_lndbg"):
            E["_lndbg"] = True
            b.wait("gpsimd", ev_bc)
            b.op("gpsimd", lambda q: q.dma_start(out=d_dbga[0:1, :], in_=ab2[0:1, 0, :]),
                 sem=sDBG, inc=16)
            b.op("gpsimd", lambda q: q.dma_start(out=d_dbga[1:2, :], in_=ab2[0:1, 1, :]),
                 sem=sDBG, inc=16)
            b.op("gpsimd", lambda q: q.dma_start(out=d_dbga[4:5, :], in_=t_trow[0:1, :]),
                 sem=sDBG, inc=16)
            b.op("gpsimd", lambda q: q.dma_start(out=d_dbga[5:6, :], in_=t_arow[0:1, :]),
                 sem=sDBG, inc=16)
        ev_ap = None
        for k in range(NDT):
            b.op("vector", lambda q, _k=k: q.tensor_tensor(
                t_tmp1[:], xa3[:, _k, :], ab2[:, 0, :], ALU.mult))
            ev_ap = b.op("vector", lambda q, _k=k: q.tensor_tensor(
                dst3[:, _k, dst_c0:dst_c0 + HALF], t_tmp1[:], ab2[:, 1, :], ALU.add), sem=sV)
        return ev_ap

    def x1x2(kind, ev_ln, wait_w=(), gate_reads=()):
        grp_ev = [None] * NFB
        ev_evac_v = [None] * NFB
        ev_evac_s = [None] * NFB
        ev_q2 = {}
        ev_sdma = {}
        prev_add = None
        wsrc = {0: d_w1T, 1: d_w2T}
        wdst = {0: d_w1q, 1: d_w2q}
        for fb in range(NFB):
            par = fb % 2
            wv = wq3 if kind == "quant" else wqb3
            if kind == "quant":
                for m in (0, 1):
                    if fb > 0:
                        b.wait("sync", ev_q2[(fb - 1, m)])
                    elif m == 0:
                        for ev in wait_w:
                            b.wait("sync", ev)
                    ev_d = b.op("sync", lambda q, _m=m, _fb=fb: q.dma_start(
                        out=wf3[:, _m, :, :],
                        in_=wsrc[_m][:, _fb * 128:(_fb + 1) * 128]
                        .rearrange("(k p) f -> p k f", p=128)),
                        sem=sWF[m], inc=16)
                    if fb == 0 and m == 0:
                        b.wait("vector", E["s_ready"])
                    if fb >= 2:
                        b.wait("vector", grp_ev[fb - 2])
                        b.wait("vector", ev_sdma[(fb - 2, m)])
                    b.wait("vector", ev_d)
                    b.op("vector", lambda q, _par=par, _m=m: q.tensor_scalar(
                        wq3[:, _par, _m, :], wf3[:, _m, :, :].rearrange("p k f -> p (k f)"),
                        t_sb[:, _m:_m + 1], 2.0, ALU.is_ge, ALU.mult))
                    b.op("vector", lambda q, _m=m: q.tensor_scalar(
                        stbf2[:, 2, :].rearrange("p n -> p n") if False else
                        t_stbf[:, 2 * HALF:4 * HALF],
                        wf3[:, _m, :, :].rearrange("p k f -> p (k f)"),
                        t_sb[:, 4 + _m:5 + _m], 2.0, ALU.is_le, ALU.mult))
                    ev_q2[(fb, m)] = b.op("vector", lambda q, _par=par, _m=m:
                                          q.tensor_tensor(
                        wq3[:, _par, _m, :], wq3[:, _par, _m, :],
                        t_stbf[:, 2 * HALF:4 * HALF], ALU.subtract), sem=sV)
                    b.wait("scalar", ev_q2[(fb, m)])
                    ev_sdma[(fb, m)] = b.op(
                        "scalar", lambda q, _par=par, _m=m, _fb=fb: q.dma_start(
                            out=wdst[_m][_fb], in_=wq3[:, _par, _m, :]),
                        sem=sWSQ[2 * par + m], inc=16)
                ev_wready = ev_q2[(fb, 1)]
            else:
                ev_wready = None
                for m in (0, 1):
                    if fb >= 2:
                        b.wait("sync", grp_ev[fb - 2])
                    elif fb == 0 and m == 0:
                        for ev in wait_w:
                            b.wait("sync", ev)
                        for ev in gate_reads:
                            b.wait("sync", ev)
                    ev_wready = b.op("sync", lambda q, _par=par, _m=m, _fb=fb: q.dma_start(
                        out=wqb3[:, _par, _m, :], in_=wdst[_m][_fb]),
                        sem=sWQB[2 * par + m], inc=16)

            b.wait("tensor", ev_wready)
            if fb == 0:
                b.wait("tensor", ev_ln)
            if fb >= 2:
                b.wait("tensor", ev_evac_v[fb - 2])
                b.wait("tensor", ev_evac_s[fb - 2])
            x1b = (0, 1) if par == 0 else (4, 5)
            x2b = (2, 3) if par == 0 else (6, 7)
            ev = None
            for ci, c in enumerate((0, 512)):
                for k in range(NDT):
                    b.op("tensor", lambda q, _par=par, _k=k, _c=c, _b0=x1b[ci], _wv=wv:
                         q.matmul(pbank(_b0), _wv[:, _par, 0, _k * 128:(_k + 1) * 128],
                                  ln3[:, _k, _c:_c + 512],
                                  start=(_k == 0), stop=(_k == NDT - 1)))
                    last = (ci == 1 and k == NDT - 1)
                    ev = b.op("tensor", lambda q, _par=par, _k=k, _c=c, _b0=x2b[ci], _wv=wv:
                              q.matmul(pbank(_b0), _wv[:, _par, 1, _k * 128:(_k + 1) * 128],
                                       ln3[:, _k, _c:_c + 512],
                                       start=(_k == 0), stop=(_k == NDT - 1)),
                              sem=sT if last else None)
            grp_ev[fb] = ev
            b.wait("scalar", grp_ev[fb])
            ev_evac_s[fb] = b.op("scalar", lambda q, _b0=x1b[0]: q.activation(
                t_tmp1[:], psA[:, _b0 * 512:_b0 * 512 + 1024], AF.Sigmoid,
                bias=t_zero[:, 0:1], scale=t_onesf[:, 0:1]), sem=sA)
            b.wait("vector", ev_evac_s[fb])
            ev_evac_v[fb] = b.op("vector", lambda q, _b0=x2b[0], _fb=fb:
                                 q.scalar_tensor_tensor(
                hid3[:, _fb, :], psA[:, _b0 * 512:_b0 * 512 + 1024], 0.25,
                t_tmp1[:], ALU.mult, ALU.mult), sem=sV)
        return ev_evac_v[NFB - 1]

    def w3_stage(kind, half, x_src, x_dst, ev_hid, gate_reads=()):
        c0 = half * HALF
        grp_ev = [None] * NDT
        ev_ev = [None] * NDT
        ev_xodma = [None] * NDT
        ev_q2 = {}
        ev_w3dma = [None] * NDT
        prev_add = None
        osem = sO if x_dst is d_out else sXO
        for eb in range(NDT):
            par = eb % 2
            if kind == "quant":
                for hh in (0, 1):
                    if eb == 0 and hh == 0:
                        b.wait("sync", ev_hid)
                    else:
                        b.wait("sync", ev_q2[(eb, 0)] if hh == 1
                               else ev_q2[(eb - 1, 1)])
                    ev_d = b.op("sync", lambda q, _eb=eb, _hh=hh: q.dma_start(
                        out=t_w3f[:].rearrange("p (k e) -> p k e", k=22),
                        in_=d_w3T[_hh * 2816:(_hh + 1) * 2816, _eb * 128:(_eb + 1) * 128]
                        .rearrange("(k p) e -> p k e", p=128)), sem=sW3F, inc=16)
                    if eb >= 2 and hh == 0:
                        b.wait("vector", grp_ev[eb - 2])
                        b.wait("vector", ev_w3dma[eb - 2])
                    b.wait("vector", ev_d)
                    b.op("vector", lambda q, _par=par, _hh=hh: q.tensor_scalar(
                        w3q3[:, _par, _hh * 2816:(_hh + 1) * 2816], t_w3f[:],
                        t_sb[:, 2:3], 2.0, ALU.is_ge, ALU.mult))
                    b.op("vector", lambda q: q.tensor_scalar(
                        t_w3tmp[:], t_w3f[:],
                        t_sb[:, 6:7], 2.0, ALU.is_le, ALU.mult))
                    ev_q2[(eb, hh)] = b.op("vector", lambda q, _par=par, _hh=hh:
                                           q.tensor_tensor(
                        w3q3[:, _par, _hh * 2816:(_hh + 1) * 2816],
                        w3q3[:, _par, _hh * 2816:(_hh + 1) * 2816],
                        t_w3tmp[:], ALU.subtract), sem=sV)
                b.wait("scalar", ev_q2[(eb, 1)])
                ev_w3dma[eb] = b.op("scalar", lambda q, _par=par, _eb=eb: q.dma_start(
                    out=d_w3q[_eb], in_=w3q3[:, _par, :]), sem=sW3SQ[par], inc=16)
                ev_wready = ev_q2[(eb, 1)]
            else:
                if eb >= 2:
                    b.wait("sync", grp_ev[eb - 2])
                elif eb == 0:
                    b.wait("sync", ev_hid)
                    for ev in gate_reads:
                        b.wait("sync", ev)
                ev_wready = b.op("sync", lambda q, _par=par, _eb=eb: q.dma_start(
                    out=w3q3[:, _par, :], in_=d_w3q[_eb]), sem=sW3R[par], inc=16)
            if eb == 0:
                b.wait("scalar", ev_hid)
            else:
                b.wait("scalar", ev_ev[eb - 1])
            ev_xin = b.op("scalar", lambda q, _eb=eb: q.dma_start(
                out=t_tmp1[:], in_=x_src[_eb * 128:(_eb + 1) * 128, c0:c0 + HALF]),
                sem=sXIN, inc=16)
            b.wait("tensor", ev_wready)
            if eb == 0:
                b.wait("tensor", ev_hid)
            if eb >= 2:
                b.wait("tensor", ev_ev[eb - 2])
            bk = 0 if par == 0 else 2
            for ci, c in enumerate((0, 512)):
                for k in range(NFB):
                    last = (ci == 1 and k == NFB - 1)
                    grp_ev[eb] = b.op("tensor", lambda q, _par=par, _k=k, _c=c, _bk=bk + ci:
                                      q.matmul(pbank(_bk),
                                               w3q3[:, _par, _k * 128:(_k + 1) * 128],
                                               hid3[:, _k, _c:_c + 512],
                                               start=(_k == 0), stop=(_k == NFB - 1)),
                                      sem=sT if last else None)
            b.wait("vector", grp_ev[eb])
            b.wait("vector", ev_xin)
            if eb > 0:
                b.wait("vector", ev_xodma[eb - 1])
            ev_ev[eb] = b.op("vector", lambda q, _bk=bk: q.tensor_tensor(
                t_xout[:], psA[:, _bk * 512:_bk * 512 + 1024], t_tmp1[:], ALU.add), sem=sV)
            b.wait("scalar", ev_ev[eb])
            ev_xodma[eb] = b.op("scalar", lambda q, _eb=eb: q.dma_start(
                out=x_dst[_eb * 128:(_eb + 1) * 128, c0:c0 + HALF], in_=t_xout[:]),
                sem=osem, inc=16)
        return ev_ev[NDT - 1], ev_xodma[NDT - 1], grp_ev[NDT - 1]

    def proj(ev_ln2_done):
        ev_grp = {}
        ev_evac = {}
        ev_edma = {}
        mats = ("q", "k", "v")
        seq = [(mm_, eb) for mm_ in mats for eb in range(NDT)]
        for i, (mm_, eb) in enumerate(seq):
            par = i % 2
            if i >= 2:
                b.wait("gpsimd", ev_grp[seq[i - 2]])
            ev_w = b.op("gpsimd", lambda q, _m=mm_, _eb=eb, _par=par: q.dma_start(
                out=wc3[:, _par, :].rearrange("p (k e) -> p k e", k=16),
                in_=d_W[_m][:, _eb * 128:(_eb + 1) * 128]
                .rearrange("(k p) e -> p k e", p=128)), sem=sWC[par], inc=16)
            b.wait("tensor", ev_w)
            if i == 0:
                b.wait("tensor", ev_ln2_done)
            if i >= 2:
                b.wait("tensor", ev_evac[seq[i - 2]])
            b0 = 0 if par == 0 else 4
            ev = None
            for ci in range(4):
                for k in range(NDT):
                    last = (ci == 3 and k == NDT - 1)
                    ev = b.op("tensor", lambda q, _par=par, _k=k, _ci=ci, _b0=b0:
                              q.matmul(pbank(_b0 + _ci),
                                       wc3[:, _par, _k * 128:(_k + 1) * 128],
                                       ln23[:, _k, _ci * 512:(_ci + 1) * 512],
                                       start=(_k == 0), stop=(_k == NDT - 1)),
                              sem=sT if last else None)
            ev_grp[(mm_, eb)] = ev
            b.wait("vector", ev)
            if i == 0:
                b.wait("vector", E["init"])
            if i >= 2:
                b.wait("vector", ev_edma[seq[i - 2]])
            pe = t_pe0 if par == 0 else t_pe1
            ev_evac[(mm_, eb)] = b.op("vector", lambda q, _m=mm_, _eb=eb, _pe=pe, _b0=b0:
                                      q.tensor_scalar_add(
                _pe[:], psA[:, _b0 * 512:(_b0 + 4) * 512],
                t_bias[_m][:, _eb:_eb + 1]), sem=sV)
            b.wait("scalar", ev_evac[(mm_, eb)])
            ev_edma[(mm_, eb)] = b.op("scalar", lambda q, _m=mm_, _eb=eb, _pe=pe:
                                      q.dma_start(
                out=d_sc[_m][_eb * 128:(_eb + 1) * 128, :], in_=_pe[:]), sem=sPE[par], inc=16)
        E["proj_last_mm"] = ev_grp[seq[-1]]
        E["proj_pe0"] = (sPE[0].h, sPE[0].n)
        E["proj_pe1"] = (sPE[1].h, sPE[1].n)

    chunk_tensor_done = [None] * NCH

    def scan():
        b.wait("vector", E["proj_last_mm"])
        b.op("vector", lambda q: q.memset(t_Sf[:], 0.0))
        ev_s0 = b.op("vector", lambda q: q.memset(t_Sb[:], 0.0), sem=sV)
        at_evac_ev = [None, None]
        out_free_ev = None
        su_free_ev = None
        out_mm_prev = None
        xdma_ev = [None] * NCH
        x2dma_ev = [None] * NCH
        chunk_done_v = [None] * NCH
        suev = None
        su_mm_ev = None
        for c in range(NCH):
            par = c % 2
            col = c * CH
            if c < 2:
                b.wait("sync", E["proj_pe0"])
                b.wait("sync", E["proj_pe1"])
            else:
                b.wait("sync", chunk_tensor_done[c - 2])
            ev_qc = b.op("sync", lambda q, _par=par, _col=col: q.dma_start(
                out=t_qc[_par][:].rearrange("p (k t) -> p k t", k=16),
                in_=d_sc["q"][:, _col:_col + CH].rearrange("(k p) t -> p k t", p=128)),
                sem=sQC[par], inc=16)
            ev_kc = b.op("sync", lambda q, _par=par, _col=col: q.dma_start(
                out=t_kc[_par][:].rearrange("p (k t) -> p k t", k=16),
                in_=d_sc["k"][:, _col:_col + CH].rearrange("(k p) t -> p k t", p=128)),
                sem=sKC[par], inc=16)
            ev_kn = ev_vn = None
            for jb in (0, 1):
                ev_kn = b.op("sync", lambda q, _par=par, _col=col, _jb=jb: q.dma_start(
                    out=kn3[_par][:, _jb, :],
                    in_=d_sc["k"][:, _col + _jb * 128:_col + (_jb + 1) * 128],
                    transpose=True), sem=sKN[par], inc=16)
                ev_vn = b.op("sync", lambda q, _par=par, _col=col, _jb=jb: q.dma_start(
                    out=vn3[_par][:, _jb, :],
                    in_=d_sc["v"][:, _col + _jb * 128:_col + (_jb + 1) * 128],
                    transpose=True), sem=sVN[par], inc=16)
            if c >= 1:
                b.wait("scalar", chunk_done_v[c - 1])
            xdma_ev[c] = b.op("scalar", lambda q, _col=col: q.dma_start(
                out=xc3, in_=d_xp[:, _col:_col + CH]
                .rearrange("(k p) t -> p k t", p=128)), sem=sXC, inc=16)
            b.wait("vector", ev_kn)
            if c == 0:
                b.wait("vector", E["attn_consts"])
                b.wait("vector", ev_s0)
            for jb in (0, 1):
                b.op("vector", lambda q, _par=par, _jb=jb: q.tensor_tensor(
                    kn3[_par][:, _jb, :], kn3[_par][:, _jb, :], kst3[:, _jb, :], ALU.mult),
                    sem=sV)
            for h in range(HEADS):
                hp = h % 2
                ab0 = 0 if hp == 0 else 2
                b.wait("vector", out_mm_prev)
                if h == 0:
                    b.wait("vector", ev_qc)
                qev = None
                for kk in (0, 1):
                    qev = b.op("vector", lambda q, _par=par, _h=h, _kk=kk: q.tensor_tensor(
                        qsh2[:, _kk, :], qc3[_par][:, 2 * _h + _kk, :],
                        t_cs[:, _h * CH:(_h + 1) * CH], ALU.mult), sem=sV)
                b.wait("tensor", qev)
                if h == 0:
                    b.wait("tensor", ev_kc)
                b.wait("tensor", at_evac_ev[hp])
                ev = None
                for jb in (0, 1):
                    for kk in (0, 1):
                        ev = b.op("tensor", lambda q, _par=par, _h=h, _jb=jb, _kk=kk,
                                  _b0=ab0: q.matmul(
                            pbank(_b0 + _jb, 256),
                            kc3[_par][:, 2 * _h + _kk, _jb * 128:(_jb + 1) * 128],
                            qc3[_par][:, 2 * _h + _kk, :],
                            start=(_kk == 0), stop=(_kk == 1)),
                            sem=sT if (jb == 1 and kk == 1) else None)
                b.wait("vector", ev)
                aev = None
                for jb in (0, 1):
                    aev = b.op("vector", lambda q, _jb=jb, _h=h, _b0=ab0: q.tensor_tensor(
                        at2[:, _jb, :], pbank(_b0 + _jb, 256),
                        dt3[:, _jb, _h * CH:(_h + 1) * CH], ALU.mult), sem=sV)
                at_evac_ev[hp] = aev
                b.wait("tensor", aev)
                b.wait("tensor", out_free_ev)
                if h == 0:
                    b.wait("tensor", ev_vn)
                    if c >= 1:
                        b.wait("tensor", chunk_done_v[c - 1])
                ev = None
                for eb in (0, 1):
                    for jb in (0, 1):
                        b.op("tensor", lambda q, _par=par, _h=h, _eb=eb, _jb=jb: q.matmul(
                            pbank(4 + _eb, 256),
                            vn3[_par][:, _jb, _h * DH + _eb * 128:_h * DH + (_eb + 1) * 128],
                            at2[:, _jb, :], start=(_jb == 0), stop=False))
                    for kk in (0, 1):
                        ev = b.op("tensor", lambda q, _h=h, _eb=eb, _kk=kk: q.matmul(
                            pbank(4 + _eb, 256),
                            Sb3[:, _kk, _h, _eb * 128:(_eb + 1) * 128],
                            qsh2[:, _kk, :], start=False, stop=(_kk == 1)),
                            sem=sT if (eb == 1 and kk == 1) else None)
                out_mm_ev = ev
                out_mm_prev = ev
                b.wait("tensor", su_free_ev)
                ev = None
                for kk in (0, 1):
                    for jb in (0, 1):
                        ev = b.op("tensor", lambda q, _par=par, _h=h, _kk=kk, _jb=jb:
                                  q.matmul(
                            pbank(6 + _kk, 256),
                            kn3[_par][:, _jb, _h * DH + _kk * 128:_h * DH + (_kk + 1) * 128],
                            vn3[_par][:, _jb, _h * DH:(_h + 1) * DH],
                            start=(_jb == 0), stop=(_jb == 1)),
                            sem=sT if (kk == 1 and jb == 1) else None)
                su_mm_ev = ev
                b.wait("vector", out_mm_ev)
                b.wait("vector", xdma_ev[c])
                if c >= 1 and h == 0:
                    b.wait("vector", x2dma_ev[c - 1])
                oev = None
                for eb in (0, 1):
                    oev = b.op("vector", lambda q, _h=h, _eb=eb: q.tensor_tensor(
                        x23[:, 2 * _h + _eb, :], pbank(4 + _eb, 256),
                        xc3[:, 2 * _h + _eb, :], ALU.add), sem=sV)
                out_free_ev = oev
                b.wait("vector", su_mm_ev)
                for kk in (0, 1):
                    b.op("vector", lambda q, _h=h, _kk=kk: q.scalar_tensor_tensor(
                        Sf3[:, _kk, _h, :], Sf3[:, _kk, _h, :], CD[_h],
                        pbank(6 + _kk, 256), ALU.mult, ALU.add))
                    suev = b.op("vector", lambda q, _h=h, _kk=kk: q.tensor_copy(
                        Sb3[:, _kk, _h, :], Sf3[:, _kk, _h, :]), sem=sV)
                su_free_ev = suev
            chunk_done_v[c] = suev
            chunk_tensor_done[c] = su_mm_ev
            b.wait("scalar", out_free_ev)
            x2dma_ev[c] = b.op("scalar", lambda q, _col=col: q.dma_start(
                out=d_xpp[:, _col:_col + CH].rearrange("(k p) t -> p k t", p=128),
                in_=x23), sem=sX2, inc=16)
        b.wait("gpsimd", chunk_done_v[NCH - 1])
        ev_ai = b.op("gpsimd", lambda q: q.dma_start(
            out=d_agin.rearrange("(k p) e -> p k e", p=128),
            in_=t_Sf[:].rearrange("p (k e) -> p k e", k=16)), sem=sCC, inc=16)
        b.wait("gpsimd", ev_ai)
        E["ag_done"] = b.op("gpsimd", lambda q: q.collective_compute(
            "AllGather", ALU.bypass,
            replica_groups=[[0, 1], [2, 3], [4, 5], [6, 7]],
            ins=[d_agin.opt()], outs=[d_agout.opt()]), sem=sCC, inc=1)
        E["scan_x2_last"] = x2dma_ev[NCH - 1]

    corr_t_done = [None] * NCH

    def corr():
        b.wait("gpsimd", E["ag_done"])
        ev_b0 = b.op("gpsimd", lambda q: q.dma_start(
            out=t_T[:].rearrange("p (k e) -> p k e", k=16),
            in_=d_agout[0:DM, :].rearrange("(k p) e -> p k e", p=128)), sem=sCC, inc=16)
        b.wait("gpsimd", E["scan_x2_last"])
        b.wait("gpsimd", ev_b0)
        ev_b1 = b.op("gpsimd", lambda q: q.dma_start(
            out=t_x2[:].rearrange("p (k e) -> p k e", k=16),
            in_=d_agout[DM:2 * DM, :].rearrange("(k p) e -> p k e", p=128)), sem=sCC, inc=16)
        b.wait("vector", ev_b1)
        b.op("vector", lambda q: q.tensor_scalar_mul(t_T[:], t_T[:], t_msk[:, 0:1]))
        ev_T = b.op("vector", lambda q: q.scalar_tensor_tensor(
            t_T[:], t_x2[:], t_msk[:, 1:2], t_T[:], ALU.mult, ALU.add), sem=sV)
        prev_v = ev_T
        prev_xdma = None
        for c in range(NCH):
            par = c % 2
            col = c * CH
            b.wait("vector", prev_v)
            ev_tb = b.op("vector", lambda q: q.tensor_copy(t_Tb[:], t_T[:]), sem=sV)
            if c >= 2:
                b.wait("sync", corr_t_done[c - 2])
            else:
                b.wait("sync", E["scan_x2_last"])
            ev_q = b.op("sync", lambda q, _par=par, _col=col: q.dma_start(
                out=t_qc[_par][:].rearrange("p (k t) -> p k t", k=16),
                in_=d_sc["q"][:, _col:_col + CH].rearrange("(k p) t -> p k t", p=128)),
                sem=sQC[par], inc=16)
            if c == 0:
                b.wait("scalar", E["scan_x2_last"])
            if prev_xdma is not None:
                b.wait("scalar", prev_xdma)
            ev_x = b.op("scalar", lambda q, _col=col: q.dma_start(
                out=xc3, in_=d_xpp[:, _col:_col + CH]
                .rearrange("(k p) t -> p k t", p=128)), sem=sXC, inc=16)
            last_t = None
            oev = None
            for h in range(HEADS):
                if h == 0:
                    b.wait("vector", ev_q)
                b.wait("vector", last_t)
                qev = None
                for kk in (0, 1):
                    qev = b.op("vector", lambda q, _par=par, _h=h, _kk=kk: q.tensor_tensor(
                        qsh2[:, _kk, :], qc3[_par][:, 2 * _h + _kk, :],
                        t_cs[:, _h * CH:(_h + 1) * CH], ALU.mult), sem=sV)
                b.wait("tensor", qev)
                if h == 0:
                    b.wait("tensor", ev_tb)
                if oev is not None:
                    b.wait("tensor", oev)
                for eb in (0, 1):
                    for kk in (0, 1):
                        last_t = b.op("tensor", lambda q, _h=h, _eb=eb, _kk=kk: q.matmul(
                            pbank(_eb, 256),
                            Tb3[:, _kk, _h, _eb * 128:(_eb + 1) * 128],
                            qsh2[:, _kk, :], start=(_kk == 0), stop=(_kk == 1)),
                            sem=sT if (eb == 1 and kk == 1) else None)
                b.wait("vector", last_t)
                if h == 0:
                    b.wait("vector", ev_x)
                for eb in (0, 1):
                    oev = b.op("vector", lambda q, _h=h, _eb=eb: q.tensor_tensor(
                        xc3[:, 2 * _h + _eb, :], pbank(_eb, 256),
                        xc3[:, 2 * _h + _eb, :], ALU.add), sem=sV)
            corr_t_done[c] = last_t
            b.wait("vector", last_t)
            tev = None
            for kk in (0, 1):
                for h in range(HEADS):
                    tev = b.op("vector", lambda q, _kk=kk, _h=h: q.tensor_scalar_mul(
                        T3[:, _kk, _h, :], T3[:, _kk, _h, :], CD[_h]), sem=sV)
            prev_v = tev
            b.wait("scalar", oev)
            prev_xdma = b.op("scalar", lambda q, _col=col: q.dma_start(
                out=d_xpp[:, _col:_col + CH].rearrange("(k p) t -> p k t", p=128),
                in_=xc3), sem=sXC, inc=16)
        E["corr_last_xdma"] = prev_xdma

    if TESTMODE:
        if TESTMODE == 1:
            ph_init()
        else:
            b.op("gpsimd", lambda q: q.dma_start(out=t_msk[:], in_=d_msk), sem=sInit, inc=16)
            E["init"] = b.op("gpsimd", lambda q: q.dma_start(
                out=t_bias["v"][:], in_=d_b["v"]), sem=sInit, inc=16)
            b.op("vector", lambda q: q.memset(t_ones[:], 1.0))
            b.op("vector", lambda q: q.memset(t_onesf[:], 1.0))
            b.op("vector", lambda q: q.memset(t_hm[:], -0.5))
            b.op("vector", lambda q: q.memset(t_zero[:], 0.0))
            E["zeros"] = b.op("vector", lambda q: q.memset(t_hp[:], 0.5), sem=sV)
            E["s_ready"] = E["zeros"]
        ev_ln = ln_block(d_x, 0, 0.5, ln3, 0)
        b.wait("gpsimd", ev_ln)
        ev_f = b.op("gpsimd", lambda q: q.dma_start(out=d_dbgl[0], in_=t_ln[:]),
                    sem=sDBG, inc=16)
        b.wait("gpsimd", ev_f)
        b.wait("scalar", ev_f)
        b.wait("sync", ev_f)
        return b.finish()

    ph_init()

    ev_ln = ln_block(d_x, 0, 0.5, ln3, 0)
    if DEBUG:
        b.wait("gpsimd", ev_ln)
        b.op("gpsimd", lambda q: q.dma_start(out=d_dbgl[0], in_=t_ln[:]), sem=sDBG, inc=16)
    ev_hid = x1x2("quant", ev_ln)
    if DEBUG:
        b.wait("gpsimd", ev_hid)
        b.op("gpsimd", lambda q: q.dma_start(out=d_dbgh[0], in_=t_hid[:]), sem=sDBG, inc=16)
    w3a = w3_stage("quant", 0, d_x, d_xp, ev_hid)
    ev_ln = ln_block(d_x, 1, 0.5, ln3, 0, wait_load=[("scalar", w3a[2])])
    sc_gates = [(sWSQ[i].h, sWSQ[i].n) for i in range(4)]
    ev_hid = x1x2("scratch", ev_ln, wait_w=[w3a[2]], gate_reads=sc_gates)
    if DEBUG:
        b.wait("gpsimd", ev_hid)
        b.op("gpsimd", lambda q: q.dma_start(out=d_dbgh[1], in_=t_hid[:]), sem=sDBG, inc=16)
        b.op("gpsimd", lambda q: q.dma_start(out=d_dbgl[1], in_=t_ln[:]), sem=sDBG, inc=16)
    w3_gates = [(sW3SQ[i].h, sW3SQ[i].n) for i in range(2)]
    w3b = w3_stage("scratch", 1, d_x, d_xp, ev_hid, gate_reads=w3_gates)

    ev_l2a = ln_block(d_xp, 0, 1.0, ln23, 0,
                      wait_load=[("scalar", w3b[2]), ("scalar", (sXO.h, sXO.n))])
    ev_l2b = ln_block(d_xp, 1, 1.0, ln23, HALF, wait_load=[("scalar", ev_l2a)])

    b.wait("gpsimd", ev_l2b)
    b.op("gpsimd", lambda q: q.dma_start(
        out=dt3, in_=d_dt.rearrange("(j p) n -> p j n", p=128)), sem=sTabs, inc=16)
    b.op("gpsimd", lambda q: q.dma_start(out=t_cs[:], in_=d_cs), sem=sTabs, inc=16)
    E["attn_consts"] = b.op("gpsimd", lambda q: q.dma_start(
        out=kst3, in_=d_kst.rearrange("(j p) n -> p j n", p=128)), sem=sTabs, inc=16)

    proj(ev_l2b)
    scan()
    corr()

    b.barrier(sBar)

    ev_ln = ln_block(d_xpp, 0, 0.5, ln3, 0,
                     wait_load=[("scalar", E["corr_last_xdma"])])
    ev_hid = x1x2("scratch", ev_ln, gate_reads=sc_gates)
    w3f2 = w3_stage("scratch", 0, d_xpp, d_out, ev_hid, gate_reads=w3_gates)
    ev_ln = ln_block(d_xpp, 1, 0.5, ln3, 0, wait_load=[("scalar", w3f2[2])])
    ev_hid = x1x2("scratch", ev_ln, wait_w=[w3f2[2]])
    w3g = w3_stage("scratch", 1, d_xpp, d_out, ev_hid)

    b.wait("scalar", (sO.h, sO.n))
    b.wait("sync", (sO.h, sO.n))

    return b.finish()


def _host_tables():
    idx = np.arange(CH)
    dt = np.zeros((CH, HEADS * CH), np.float32)
    cs = np.zeros((128, HEADS * CH), np.float32)
    kst = np.zeros((CH, DM), np.float32)
    for h in range(HEADS):
        g = _g[h]
        D = np.where(idx[:, None] >= idx[None, :],
                     g ** (idx[:, None] - idx[None, :]).astype(np.float64), 0.0)
        dt[:, h * CH:(h + 1) * CH] = D.T.astype(np.float32)
        cs[:, h * CH:(h + 1) * CH] = np.broadcast_to(
            (g ** (idx + 1).astype(np.float64)).astype(np.float32), (128, CH))
        kst[:, h * DH:(h + 1) * DH] = np.broadcast_to(
            (g ** (CH - 1 - idx).astype(np.float64)).astype(np.float32)[:, None],
            (CH, DH))
    return dt, cs, kst


_CACHE = {}


def kernel(L0_attn_x, L0_attn_Wq, L0_attn_bq, L0_attn_Wk, L0_attn_bk,
           L0_attn_Wv, L0_attn_bv, L0_ffn_w1, L0_ffn_w2, L0_ffn_w3):
    x = np.ascontiguousarray(np.asarray(L0_attn_x, np.float32))
    Wq = np.ascontiguousarray(np.asarray(L0_attn_Wq, np.float32))
    Wk = np.ascontiguousarray(np.asarray(L0_attn_Wk, np.float32))
    Wv = np.ascontiguousarray(np.asarray(L0_attn_Wv, np.float32))
    bq = np.asarray(L0_attn_bq, np.float32)
    bk = np.asarray(L0_attn_bk, np.float32)
    bv = np.asarray(L0_attn_bv, np.float32)
    w1 = np.asarray(L0_ffn_w1, np.float32)
    w2 = np.asarray(L0_ffn_w2, np.float32)
    w3 = np.asarray(L0_ffn_w3, np.float32)

    w1T = np.ascontiguousarray(w1.T)
    w2T = np.ascontiguousarray(w2.T)
    w3T = np.ascontiguousarray(w3.T)

    dt, cs, kst = _host_tables()
    bq_r = np.ascontiguousarray(bq.reshape(16, 128).T)
    bk_r = np.ascontiguousarray(bk.reshape(16, 128).T)
    bv_r = np.ascontiguousarray(bv.reshape(16, 128).T)

    if "nc" not in _CACHE:
        _CACHE["nc"] = build_program()
    nc = _CACHE["nc"]

    in_maps = []
    for c in range(8):
        bb, hh = divmod(c, 2)
        xT = np.ascontiguousarray(x[bb, hh * TOK:(hh + 1) * TOK, :].T)
        wsh = np.empty((3, 128, 11264), np.float32)
        wsh[0] = w1T[256 * c:256 * (c + 1), :].reshape(2, 128, FH) \
            .transpose(1, 0, 2).reshape(128, 2 * FH)
        wsh[1] = w2T[256 * c:256 * (c + 1), :].reshape(2, 128, FH) \
            .transpose(1, 0, 2).reshape(128, 2 * FH)
        wsh[2] = w3T[:, 256 * c:256 * (c + 1)].reshape(NFB, 128, 256) \
            .transpose(1, 0, 2).reshape(128, NFB * 256)
        cmask = np.zeros((128, 2), np.float32)
        if hh == 1:
            cmask[:, 0] = 1.0
        in_maps.append({
            "xT": xT, "w1T": w1T, "w2T": w2T, "w3T": w3T,
            "Wq": Wq, "Wk": Wk, "Wv": Wv, "wsh": wsh,
            "bq_r": bq_r, "bk_r": bk_r, "bv_r": bv_r,
            "dt_tab": dt, "cs_tab": cs, "ks_tab": kst, "cmask": cmask,
        })

    res = run_bass_kernel_spmd(nc, in_maps, core_ids=list(range(8)))
    _CACHE["res"] = res
    out = np.empty((4, 4096, DM), np.float32)
    for c in range(8):
        bb, hh = divmod(c, 2)
        out[bb, hh * TOK:(hh + 1) * TOK, :] = res.results[c]["outT"].T
    return out


# version 10 (final)
# speedup vs baseline: 1.1632x; 1.1394x over previous
import numpy as np
import ml_dtypes

import concourse.bass as bass
import concourse.mybir as mybir
from concourse.bass_utils import run_bass_kernel_spmd

F32 = mybir.dt.float32
BF16 = mybir.dt.bfloat16
AF = mybir.ActivationFunctionType
ALU = mybir.AluOpType
AX = mybir.AxisListType.X
BF = ml_dtypes.bfloat16

DM = 2048
TOK = 2048
HALF = 1024
FH = 5632
NFB = FH // 128
NDT = DM // 128
HEADS = 8
DH = 256
CH = 256
NCH = TOK // CH
WCNT = float(FH * DM)
EPSQ = 1e-8
EPSLN = 1e-5
KB = 1024

ENGS = ("tensor", "vector", "scalar", "gpsimd", "sync")
DEBUG = False
TESTMODE = 0

_betas = np.linspace(np.log2(1.0 / 32.0), np.log2(1.0 / 512.0), HEADS)
_g = 1.0 - np.exp2(_betas)
CD = [float(v) for v in (_g ** CH)]


class _Sem:
    __slots__ = ("h", "n")

    def __init__(self, h):
        self.h = h
        self.n = 0


class B:
    def __init__(self):
        self.nc = bass.Bass("TRN2", target_bir_lowering=False)
        self.prog = {e: [] for e in ENGS}
        self._ctxs = []

    def sem(self, name):
        ctx = self.nc.semaphore(name)
        h = ctx.__enter__()
        self._ctxs.append(ctx)
        return _Sem(h)

    def op(self, eng, build, sem=None, inc=1):
        if sem is None:
            self.prog[eng].append(build)
            return None

        def f(q, _b=build, _s=sem.h, _n=inc):
            _b(q).then_inc(_s, _n)

        self.prog[eng].append(f)
        sem.n += inc
        return (sem.h, sem.n)

    def wait(self, eng, ev):
        if ev is None:
            return
        h, v = ev
        self.prog[eng].append(lambda q, _h=h, _v=v: q.wait_ge(_h, _v))

    def chain(self, eng, sem, builds, first_ev=None):
        ev = first_ev
        for f in builds:
            if ev is not None:
                self.wait(eng, ev)
            ev = self.op(eng, f, sem=sem)
        return ev

    def barrier(self, bsem):
        for e in ENGS:
            self.op(e, lambda q: q.nop(), sem=bsem, inc=1)
        tgt = bsem.n
        for e in ENGS:
            self.wait(e, (bsem.h, tgt))

    def finish(self):
        nc = self.nc
        with nc.Block() as block:
            @block.tensor
            def _(q):
                for f in self.prog["tensor"]:
                    f(q)

            @block.vector
            def _(q):
                for f in self.prog["vector"]:
                    f(q)

            @block.scalar
            def _(q):
                for f in self.prog["scalar"]:
                    f(q)

            @block.gpsimd
            def _(q):
                for f in self.prog["gpsimd"]:
                    f(q)

            @block.sync
            def _(q):
                for f in self.prog["sync"]:
                    f(q)
        for ctx in self._ctxs:
            ctx.__exit__(None, None, None)
        return nc


def build_program():
    b = B()
    nc = b.nc

    d_x = nc.dram_tensor("xT", [DM, TOK], F32, kind="ExternalInput").ap()
    d_w1T = nc.dram_tensor("w1T", [DM, FH], F32, kind="ExternalInput").ap()
    d_w2T = nc.dram_tensor("w2T", [DM, FH], F32, kind="ExternalInput").ap()
    d_w3T = nc.dram_tensor("w3T", [FH, DM], F32, kind="ExternalInput").ap()
    d_W = {
        "q": nc.dram_tensor("Wq", [DM, DM], F32, kind="ExternalInput").ap(),
        "k": nc.dram_tensor("Wk", [DM, DM], F32, kind="ExternalInput").ap(),
        "v": nc.dram_tensor("Wv", [DM, DM], F32, kind="ExternalInput").ap(),
    }
    d_wsh = nc.dram_tensor("wsh", [3, 128, 11264], F32, kind="ExternalInput").ap()
    d_b = {
        "q": nc.dram_tensor("bq_r", [128, 16], F32, kind="ExternalInput").ap(),
        "k": nc.dram_tensor("bk_r", [128, 16], F32, kind="ExternalInput").ap(),
        "v": nc.dram_tensor("bv_r", [128, 16], F32, kind="ExternalInput").ap(),
    }
    d_dt = nc.dram_tensor("dt_tab", [CH, HEADS * CH], F32, kind="ExternalInput").ap()
    d_cs = nc.dram_tensor("cs_tab", [128, HEADS * CH], F32, kind="ExternalInput").ap()
    d_kst = nc.dram_tensor("ks_tab", [CH, DM], F32, kind="ExternalInput").ap()
    d_msk = nc.dram_tensor("cmask", [128, 2], F32, kind="ExternalInput").ap()

    d_out = nc.dram_tensor("outT", [DM, TOK], F32, kind="ExternalOutput").ap()
    d_xp = nc.dram_tensor("xp", [DM, TOK], F32, kind="ExternalOutput").ap()
    d_xpp = nc.dram_tensor("xpp", [DM, TOK], F32, kind="ExternalOutput").ap()
    _dk = "ExternalOutput" if DEBUG else "Internal"
    d_w1q = nc.dram_tensor("w1q_s", [NFB, 128, DM], BF16, kind=_dk).ap()
    d_w2q = nc.dram_tensor("w2q_s", [NFB, 128, DM], BF16, kind=_dk).ap()
    d_w3q = nc.dram_tensor("w3q_s", [NDT, 128, FH], BF16, kind=_dk).ap()
    d_sc = {
        "q": nc.dram_tensor("qT_s", [DM, TOK], BF16, kind="Internal").ap(),
        "k": nc.dram_tensor("kT_s", [DM, TOK], BF16, kind="Internal").ap(),
        "v": nc.dram_tensor("vT_s", [DM, TOK], BF16, kind="Internal").ap(),
    }
    d_ab = nc.dram_tensor("ab_s", [2, HALF], F32, kind="Internal").ap()
    d_gs = nc.dram_tensor("g_s", [1, 8], F32, kind="Internal").ap()
    d_gin = nc.dram_tensor("g_in", [128, 128], F32, kind="Internal").ap()
    d_gout = nc.dram_tensor("g_out", [128, 128], F32, kind="Internal").ap()
    d_agin = nc.dram_tensor("ag_in", [DM, DH], F32, kind="Internal").ap()
    d_agout = nc.dram_tensor("ag_out", [2 * DM, DH], F32, kind="Internal").ap()
    if DEBUG:
        d_dbg = nc.dram_tensor("dbg", [128, 64], F32, kind="ExternalOutput").ap()
        d_dbgb = nc.dram_tensor("dbgb", [128, DM], BF16, kind="ExternalOutput").ap()
        d_dbgh = nc.dram_tensor("dbgh", [2, 128, NFB * HALF], BF16,
                                kind="ExternalOutput").ap()
        d_dbga = nc.dram_tensor("dbga", [6, HALF], F32, kind="ExternalOutput").ap()
        d_dbgr = nc.dram_tensor("dbgr", [12, HALF], F32, kind="ExternalOutput").ap()
        d_dbgl = nc.dram_tensor("dbgl", [2, 128, NDT * HALF], BF16,
                                kind="ExternalOutput").ap()

    SB0 = 16 * KB

    def A(name, shape, dtype, offset):
        return nc.alloc_sbuf_tensor_at(name, shape, dtype, offset=SB0 + offset)

    t_ones = A("ones_bf", [128, 2], BF16, offset=0)
    t_onesf = A("ones_f", [128, 2], F32, offset=32)
    t_sb = A("s_b", [128, 8], F32, offset=64)
    t_msk = A("cmask_t", [128, 2], F32, offset=96)
    t_gacc = A("gacc", [128, 4], F32, offset=128)
    t_hm = A("half_m", [128, 1], F32, offset=160)
    t_hp = A("half_p", [128, 1], F32, offset=192)
    t_zero = A("zero_t", [128, 1], F32, offset=208 if False else 13 * KB)
    t_bias = {
        "q": A("bq_t", [128, 16], F32, offset=224),
        "k": A("bk_t", [128, 16], F32, offset=288),
        "v": A("bv_t", [128, 16], F32, offset=352),
    }
    t_gbuf = A("gbuf", [128, 128], F32, offset=416)
    t_grow = A("g_row", [1, 8], F32, offset=928)
    t_arow = A("a_row", [1, HALF], F32, offset=960)
    t_brow = A("b_row", [1, HALF], F32, offset=960 + 4 * HALF)
    t_trow = A("t_row", [1, HALF], F32, offset=960 + 8 * HALF)

    t_ab = A("ab_b", [128, 2 * HALF], F32, offset=14 * KB)
    t_dt = A("dt_tab_t", [128, 2 * HEADS * CH], F32, offset=14 * KB)
    t_cs = A("cs_tab_t", [128, HEADS * CH], F32, offset=30 * KB)
    t_kst = A("ks_tab_t", [128, 2 * DM], F32, offset=38 * KB)
    t_wf = A("w_f32", [128, 4 * 1024], F32, offset=22 * KB)
    t_wq = A("w_q", [128, 4 * DM], BF16, offset=38 * KB)
    t_wqb = A("w_qb", [128, 4 * DM], BF16, offset=22 * KB)
    t_w3f = A("w3_f32", [128, 2 * 11 * 128], F32, offset=22 * KB)
    t_w3q = A("w3_q", [128, 2 * NFB * 128], BF16, offset=33 * KB)
    t_wc = A("w_cast", [128, 2 * NDT * 128], BF16, offset=55 * KB)
    t_w3tmp = A("w3_tmp", [128, 11 * 128], BF16, offset=55 * KB)

    OFF_R1 = 63 * KB
    t_xa = A("xA", [128, NDT * HALF], F32, offset=OFF_R1)
    t_ln = A("lnTb", [128, NDT * HALF], BF16, offset=OFF_R1)
    t_Sf = A("S_f32", [128, 2 * HEADS * DH], F32, offset=OFF_R1)
    t_Sb = A("S_bf", [128, 2 * HEADS * DH], BF16, offset=OFF_R1 + 16 * KB)
    t_T = A("T_f32", [128, NDT * CH], F32, offset=OFF_R1)
    t_Tb = A("T_bf", [128, NDT * CH], BF16, offset=OFF_R1 + 16 * KB)
    t_xc = A("x_chunk", [128, NDT * CH], F32, offset=OFF_R1 + 24 * KB)
    t_x2 = A("x2_chunk", [128, NDT * CH], F32, offset=OFF_R1 + 40 * KB)

    t_hid = A("hidden", [128, NFB * HALF], BF16, offset=95 * KB)

    OFF_R2 = 127 * KB
    t_ln2 = A("ln2Tb", [128, NDT * TOK], BF16, offset=OFF_R2)
    t_gsh = A("g_shard", [128, 2 * 5632], F32, offset=OFF_R2)
    t_qc = [A("qc0", [128, NDT * CH], BF16, offset=OFF_R2),
            A("qc1", [128, NDT * CH], BF16, offset=OFF_R2 + 32 * KB)]
    t_kc = [A("kc0", [128, NDT * CH], BF16, offset=OFF_R2 + 8 * KB),
            A("kc1", [128, NDT * CH], BF16, offset=OFF_R2 + 40 * KB)]
    t_kn = [A("kn0", [128, 2 * DM], BF16, offset=OFF_R2 + 16 * KB),
            A("kn1", [128, 2 * DM], BF16, offset=OFF_R2 + 48 * KB)]
    t_vn = [A("vn0", [128, 2 * DM], BF16, offset=OFF_R2 + 24 * KB),
            A("vn1", [128, 2 * DM], BF16, offset=OFF_R2 + 56 * KB)]

    t_tmp1 = A("tmp1", [128, HALF], F32, offset=191 * KB)
    t_pe0 = A("pe0", [128, TOK], BF16, offset=191 * KB)
    t_xout = A("xout", [128, HALF], F32, offset=195 * KB)
    t_xout2q = A("xout2q", [128, HALF], F32, offset=58 * KB)
    t_xout2s = A("xout2s", [128, HALF], F32, offset=22 * KB)
    t_pe1 = A("pe1", [128, TOK], BF16, offset=195 * KB)
    t_stbf = A("st_bf", [128, 4 * HALF], BF16, offset=199 * KB)
    t_qsh = A("qs_h", [128, 2 * CH], BF16, offset=OFF_R1 + 56 * KB)
    t_at = A("attnS", [128, 2 * CH], BF16, offset=OFF_R1 + 57 * KB)

    ps = nc.alloc_psum_tensor("ps", [128, 4096], F32)
    psA = ps.ap()

    def pbank(i, w=512):
        return psA[:, 512 * i:512 * i + w]

    def v3(t, k, n):
        return t.ap().rearrange("p (k n) -> p k n", k=k, n=n)

    ln3 = v3(t_ln, NDT, HALF)
    ln23 = v3(t_ln2, NDT, TOK)
    xa3 = v3(t_xa, NDT, HALF)
    hid3 = v3(t_hid, NFB, HALF)
    wf3 = t_wf.ap().rearrange("p (m k f) -> p m k f", m=2, k=NDT, f=128)
    wq3 = t_wq.ap().rearrange("p (pp m n) -> p pp m n", pp=2, m=2, n=DM)
    wqb3 = t_wqb.ap().rearrange("p (pp m n) -> p pp m n", pp=2, m=2, n=DM)
    w3q3 = t_w3q.ap().rearrange("p (pp n) -> p pp n", pp=2, n=NFB * 128)
    wc3 = t_wc.ap().rearrange("p (pp n) -> p pp n", pp=2, n=NDT * 128)
    ab2 = v3(t_ab, 2, HALF)
    stbf2 = v3(t_stbf, 4, HALF)
    gsh2 = v3(t_gsh, 2, 5632)
    dt3 = v3(t_dt, 2, HEADS * CH)
    kst3 = v3(t_kst, 2, DM)
    Sf3 = t_Sf.ap().rearrange("p (k h e) -> p k h e", k=2, h=HEADS, e=DH)
    Sb3 = t_Sb.ap().rearrange("p (k h e) -> p k h e", k=2, h=HEADS, e=DH)
    T3 = t_T.ap().rearrange("p (k h e) -> p k h e", k=2, h=HEADS, e=DH)
    Tb3 = t_Tb.ap().rearrange("p (k h e) -> p k h e", k=2, h=HEADS, e=DH)
    xc3 = v3(t_xc, NDT, CH)
    x23 = v3(t_x2, NDT, CH)
    at2 = v3(t_at, 2, CH)
    qsh2 = v3(t_qsh, 2, CH)
    qc3 = [v3(t, NDT, CH) for t in t_qc]
    kc3 = [v3(t, NDT, CH) for t in t_kc]
    kn3 = [v3(t, 2, DM) for t in t_kn]
    vn3 = [v3(t, 2, DM) for t in t_vn]
    lnq = t_ln.ap()

    sT = b.sem("sT")
    sV = b.sem("sV")
    sA = b.sem("sA")
    sBar = b.sem("sBar")
    sO = [b.sem(f"sO{i}") for i in range(2)]
    sInit = b.sem("sInit")
    sTabs = b.sem("sTabs")
    sGM = b.sem("sGM")
    sCC = b.sem("sCC")
    sGSH = [b.sem(f"sGSH{i}") for i in range(2)]
    sWF = [b.sem(f"sWF{i}") for i in range(4)]
    sWSQ = [b.sem(f"sWSQ{i}") for i in range(4)]
    sWQB = [b.sem(f"sWQB{i}") for i in range(4)]
    sW3F = [b.sem(f"sW3F{i}") for i in range(2)]
    sW3SQ = [b.sem(f"sW3SQ{i}") for i in range(2)]
    sW3R = [b.sem(f"sW3R{i}") for i in range(2)]
    sXA = b.sem("sXA")
    sXAq = [b.sem(f"sXAq{i}") for i in range(4)]
    sAB = b.sem("sAB")
    sXIN = b.sem("sXIN")
    sXO = [b.sem(f"sXO{i}") for i in range(2)]
    sWC = [b.sem(f"sWC{i}") for i in range(2)]
    sPE = [b.sem(f"sPE{i}") for i in range(2)]
    sQC = [b.sem(f"sQC{i}") for i in range(2)]
    sKC = [b.sem(f"sKC{i}") for i in range(2)]
    sKN = [b.sem(f"sKN{i}") for i in range(2)]
    sVN = [b.sem(f"sVN{i}") for i in range(2)]
    sXC = b.sem("sXC")
    sX2 = b.sem("sX2")
    if DEBUG:
        sDBG = b.sem("sDBG")

    E = {}

    def ph_init():
        b.op("gpsimd", lambda q: q.dma_start(out=t_msk[:], in_=d_msk), sem=sInit, inc=16)
        b.op("gpsimd", lambda q: q.dma_start(out=t_bias["q"][:], in_=d_b["q"]), sem=sInit, inc=16)
        b.op("gpsimd", lambda q: q.dma_start(out=t_bias["k"][:], in_=d_b["k"]), sem=sInit, inc=16)
        E["init"] = b.op("gpsimd", lambda q: q.dma_start(out=t_bias["v"][:], in_=d_b["v"]),
                         sem=sInit, inc=16)
        b.op("vector", lambda q: q.memset(t_ones[:], 1.0))
        b.op("vector", lambda q: q.memset(t_onesf[:], 1.0))
        b.op("vector", lambda q: q.memset(t_gbuf[:], 0.0))
        b.op("vector", lambda q: q.memset(t_hm[:], -0.5))
        b.op("vector", lambda q: q.memset(t_hp[:], 0.5))
        b.op("vector", lambda q: q.memset(t_zero[:], 0.0))
        E["zeros"] = b.op("vector", lambda q: q.memset(t_gacc[:], 0.0), sem=sV)

        red_evs = []
        for i in range(6):
            m, c = divmod(i, 2)
            if i >= 2:
                b.wait("sync", red_evs[i - 2])
            ev_d = b.op("sync", lambda q, _m=m, _c=c: q.dma_start(
                out=gsh2[:, _c, :], in_=d_wsh[_m, :, _c * 5632:(_c + 1) * 5632]),
                sem=sGSH[c], inc=16)
            if i == 0:
                b.wait("vector", E["zeros"])
            b.wait("vector", ev_d)
            ev_r = b.op("vector", lambda q, _c=c: q.reduce_sum(
                t_gacc[:, 3:4], gsh2[:, _c, :], axis=AX, apply_absolute_value=True), sem=sV)
            b.wait("vector", ev_r)
            red_evs.append(b.op("vector", lambda q, _m=m: q.tensor_tensor(
                t_gacc[:, _m:_m + 1], t_gacc[:, _m:_m + 1], t_gacc[:, 3:4], ALU.add),
                sem=sV))
        b.wait("tensor", red_evs[-1])
        ev_mm = b.op("tensor", lambda q: q.matmul(
            psA[0:1, 0:3], t_onesf[:, 0:1], t_gacc[:, 0:3], start=True, stop=True), sem=sT)
        b.wait("vector", ev_mm)
        ev_gb = b.op("vector", lambda q: q.tensor_copy(t_gbuf[0:1, 0:3], psA[0:1, 0:3]),
                     sem=sV)
        b.wait("gpsimd", ev_gb)
        ev_gi = b.op("gpsimd", lambda q: q.dma_start(out=d_gin, in_=t_gbuf[:]),
                     sem=sGM, inc=16)
        b.wait("gpsimd", ev_gi)
        E["ar"] = b.op("gpsimd", lambda q: q.collective_compute(
            "AllReduce", ALU.add, replica_groups=[list(range(8))],
            ins=[d_gin.opt()], outs=[d_gout.opt()]), sem=sGM, inc=1)

    def ph_init_tail():
        b.wait("gpsimd", E["ar"])
        ev_gr = b.op("gpsimd", lambda q: q.dma_start(out=t_grow[:], in_=d_gout[0:1, 0:8]),
                     sem=sGM, inc=16)
        b.wait("vector", ev_gr)
        ev_s = b.chain("vector", sV, [
            lambda q: q.tensor_scalar(t_grow[0:1, 4:7], t_grow[0:1, 0:3],
                                      1.0 / WCNT, EPSQ, ALU.mult, ALU.add),
            lambda q: q.tensor_scalar_mul(t_grow[0:1, 0:3], t_grow[0:1, 4:7], 0.5),
            lambda q: q.tensor_scalar_mul(t_grow[0:1, 4:7], t_grow[0:1, 0:3], -1.0),
        ])
        b.wait("gpsimd", ev_s)
        ev_so = b.op("gpsimd", lambda q: q.dma_start(out=d_gs[0:1, 0:8], in_=t_grow[0:1, 0:8]),
                     sem=sGM, inc=16)
        b.wait("gpsimd", ev_so)
        E["s_ready"] = b.op("gpsimd", lambda q: q.dma_start(
            out=t_sb[:, 0:8], in_=d_gs[0:1, 0:8].partition_broadcast(128)), sem=sGM, inc=16)
        if DEBUG:
            b.wait("gpsimd", E["s_ready"])
            b.op("gpsimd", lambda q: q.dma_start(out=d_dbg[:, 0:8], in_=t_sb[:]),
                 sem=sDBG, inc=16)
            b.op("gpsimd", lambda q: q.dma_start(out=d_dbg[:, 4:8], in_=t_gacc[:]),
                 sem=sDBG, inc=16)
            b.op("gpsimd", lambda q: q.dma_start(out=d_dbg[0:1, 8:16], in_=t_grow[:]),
                 sem=sDBG, inc=16)
            b.op("gpsimd", lambda q: q.dma_start(out=d_dbg[0:1, 16:24],
                                                 in_=t_gbuf[0:1, 0:8]), sem=sDBG, inc=16)

    def ln_block(x_src, half, scale, dst3, dst_c0, wait_load=()):
        c0 = half * HALF
        for eng, ev in wait_load:
            b.wait(eng, ev)
        ev_xq = []
        for qq in range(4):
            ev_xq.append(b.op("scalar", lambda q, _q=qq: q.dma_start(
                out=xa3[:, 4 * _q:4 * (_q + 1), :],
                in_=x_src[512 * _q:512 * (_q + 1), c0:c0 + HALF]
                .rearrange("(k p) t -> p k t", p=128)),
                sem=sXAq[qq], inc=16))
        mb = psA[0:1, 4 * 512:4 * 512 + HALF]
        qb = psA[0:1, 6 * 512:6 * 512 + HALF]
        ev_mm = None
        mm_ev = [None] * NDT
        for k in range(NDT):
            sl = 2 * (k % 2)
            if k % 4 == 0:
                b.wait("vector", ev_xq[k // 4])
                b.wait("scalar", ev_xq[k // 4])
            if k >= 2:
                b.wait("vector", mm_ev[k - 2])
                b.wait("scalar", mm_ev[k - 2])
            ev_cb = b.op("vector", lambda q, _k=k, _sl=sl: q.tensor_copy(
                stbf2[:, _sl, :], xa3[:, _k, :]), sem=sV)
            ev_sq = b.op("scalar", lambda q, _k=k, _sl=sl: q.activation(
                stbf2[:, _sl + 1, :], xa3[:, _k, :], AF.Square,
                bias=t_zero[:, 0:1], scale=t_onesf[:, 0:1]), sem=sA)
            b.wait("tensor", ev_cb)
            b.wait("tensor", ev_sq)
            for c in (0, 512):
                b.op("tensor", lambda q, _k=k, _c=c, _sl=sl: q.matmul(
                    mb[:, _c:_c + 512], t_ones[:, 0:1], stbf2[:, _sl, _c:_c + 512],
                    start=(_k == 0), stop=(_k == NDT - 1)))
                ev_mm = b.op("tensor", lambda q, _k=k, _c=c, _sl=sl: q.matmul(
                    qb[:, _c:_c + 512], t_ones[:, 0:1], stbf2[:, _sl + 1, _c:_c + 512],
                    start=(_k == 0), stop=(_k == NDT - 1)),
                    sem=sT if c == 512 else None)
            mm_ev[k] = ev_mm
        b.wait("vector", ev_mm)
        if DEBUG and not E.get("_lndbg"):
            ev1 = b.op("vector", lambda q: q.tensor_copy(t_trow[0:1, :], mb), sem=sV)
            b.wait("gpsimd", ev1)
            ev2 = b.op("gpsimd", lambda q: q.dma_start(out=d_dbga[2:3, :],
                                                       in_=t_trow[0:1, :]), sem=sDBG, inc=16)
            b.wait("vector", ev2)
            ev3 = b.op("vector", lambda q: q.tensor_copy(t_trow[0:1, :], qb), sem=sV)
            b.wait("gpsimd", ev3)
            ev4 = b.op("gpsimd", lambda q: q.dma_start(out=d_dbga[3:4, :],
                                                       in_=t_trow[0:1, :]), sem=sDBG, inc=16)
            b.wait("vector", ev4)
        dbg_on = DEBUG and not E.get("_lnrdbg")
        if dbg_on:
            E["_lnrdbg"] = True

        def _dump(row, src_t):
            if not dbg_on:
                return
            ev = b.op("vector", lambda q: q.nop(), sem=sV)
            b.wait("gpsimd", ev)
            evd = b.op("gpsimd", lambda q, _r=row, _s=src_t: q.dma_start(
                out=d_dbgr[_r:_r + 1, :], in_=_s[0:1, :]), sem=sGM, inc=16)
            b.wait("vector", evd)

        steps = [
            (lambda q: q.tensor_scalar_mul(t_arow[0:1, :], mb, 1.0 / DM), 0, t_arow),
            (lambda q: q.tensor_scalar_mul(t_brow[0:1, :], qb, 1.0 / DM), 1, t_brow),
            (lambda q: q.tensor_tensor(t_trow[0:1, :], t_arow[0:1, :], t_arow[0:1, :],
                                       ALU.mult), 2, t_trow),
            (lambda q: q.tensor_tensor(t_brow[0:1, :], t_brow[0:1, :], t_trow[0:1, :],
                                       ALU.subtract), 3, t_brow),
            (lambda q: q.tensor_scalar(t_trow[0:1, :], t_brow[0:1, :], 1.0, EPSLN,
                                       ALU.mult, ALU.add), 4, t_trow),
        ]
        ev1 = None
        for f, row, srct in steps:
            if ev1 is not None:
                b.wait("vector", ev1)
            ev1 = b.op("vector", f, sem=sV)
            _dump(row, srct)
        b.wait("scalar", ev1)
        ev_l = b.op("scalar", lambda q: q.activation(
            t_brow[0:1, :], t_trow[0:1, :], AF.Ln,
            bias=t_zero[0:1, 0:1], scale=t_onesf[0:1, 0:1]), sem=sA)
        b.wait("scalar", ev_l)
        ev_rs = b.op("scalar", lambda q: q.activation(
            t_trow[0:1, :], t_brow[0:1, :], AF.Exp,
            bias=t_zero[0:1, 0:1], scale=t_hm[0:1, 0:1]), sem=sA)
        b.wait("vector", ev_rs)
        _dump(6, t_trow)
        steps2 = [
            (lambda q: q.tensor_scalar_mul(t_trow[0:1, :], t_trow[0:1, :], float(scale)),
             7, t_trow),
            (lambda q: q.tensor_tensor(t_arow[0:1, :], t_arow[0:1, :], t_trow[0:1, :],
                                       ALU.mult), 8, t_arow),
            (lambda q: q.tensor_scalar_mul(t_arow[0:1, :], t_arow[0:1, :], -1.0),
             9, t_arow),
        ]
        ev2 = ev_rs
        for f, row, srct in steps2:
            b.wait("vector", ev2)
            ev2 = b.op("vector", f, sem=sV)
            _dump(row, srct)
        b.wait("scalar", ev2)
        b.op("scalar", lambda q: q.dma_start(out=d_ab[0:1, :], in_=t_trow[0:1, :]),
             sem=sAB, inc=16)
        ev_w = b.op("scalar", lambda q: q.dma_start(out=d_ab[1:2, :], in_=t_arow[0:1, :]),
                    sem=sAB, inc=16)
        b.wait("scalar", ev_w)
        b.op("scalar", lambda q: q.dma_start(
            out=ab2[:, 0, :], in_=d_ab[0:1, :].partition_broadcast(128)), sem=sAB, inc=16)
        ev_bc = b.op("scalar", lambda q: q.dma_start(
            out=ab2[:, 1, :], in_=d_ab[1:2, :].partition_broadcast(128)), sem=sAB, inc=16)
        b.wait("vector", ev_bc)
        if DEBUG and not E.get("_lndbg"):
            E["_lndbg"] = True
            b.wait("gpsimd", ev_bc)
            b.op("gpsimd", lambda q: q.dma_start(out=d_dbga[0:1, :], in_=ab2[0:1, 0, :]),
                 sem=sDBG, inc=16)
            b.op("gpsimd", lambda q: q.dma_start(out=d_dbga[1:2, :], in_=ab2[0:1, 1, :]),
                 sem=sDBG, inc=16)
            b.op("gpsimd", lambda q: q.dma_start(out=d_dbga[4:5, :], in_=t_trow[0:1, :]),
                 sem=sDBG, inc=16)
            b.op("gpsimd", lambda q: q.dma_start(out=d_dbga[5:6, :], in_=t_arow[0:1, :]),
                 sem=sDBG, inc=16)
        ev_ap = None
        for k in range(NDT):
            b.op("vector", lambda q, _k=k: q.tensor_tensor(
                t_tmp1[:], xa3[:, _k, :], ab2[:, 0, :], ALU.mult))
            ev_ap = b.op("vector", lambda q, _k=k: q.tensor_tensor(
                dst3[:, _k, dst_c0:dst_c0 + HALF], t_tmp1[:], ab2[:, 1, :], ALU.add), sem=sV)
        return ev_ap

    def x1x2(kind, ev_ln, wait_w=(), gate_reads=()):
        grp_ev = [None] * NFB
        ev_evac_v = [None] * NFB
        ev_evac_s = [None] * NFB
        ev_q = {}
        ev_d = {}
        ev_sdma = {}
        wsrc = {0: d_w1T, 1: d_w2T}
        wdst = {0: d_w1q, 1: d_w2q}
        wv = wq3 if kind == "quant" else wqb3

        def emit_dma(j):
            if j >= NFB:
                return
            par = j % 2
            for m in (0, 1):
                if kind == "quant":
                    if j >= 1:
                        b.wait("sync", ev_q[(j - 1, m)])
                    elif m == 0:
                        for ev in wait_w:
                            b.wait("sync", ev)
                    ev_d[(j, m)] = b.op("sync", lambda q, _m=m, _j=j: q.dma_start(
                        out=wf3[:, _m, :, :],
                        in_=wsrc[_m][:, _j * 128:(_j + 1) * 128]
                        .rearrange("(k p) f -> p k f", p=128)),
                        sem=sWF[m], inc=16)
                else:
                    if j >= 2:
                        b.wait("sync", grp_ev[j - 2])
                    elif j == 0 and m == 0:
                        for ev in wait_w:
                            b.wait("sync", ev)
                        for ev in gate_reads:
                            b.wait("sync", ev)
                    ev_d[(j, m)] = b.op("sync", lambda q, _par=par, _m=m, _j=j:
                                        q.dma_start(
                        out=wqb3[:, _par, _m, :], in_=wdst[_m][_j]),
                        sem=sWQB[2 * par + m], inc=16)

        def emit_quant(j):
            if kind != "quant" or j >= NFB:
                return
            par = j % 2
            for m in (0, 1):
                if j == 0 and m == 0:
                    b.wait("vector", E["s_ready"])
                if j >= 2:
                    b.wait("vector", grp_ev[j - 2])
                    b.wait("vector", ev_sdma[(j - 2, m)])
                b.wait("vector", ev_d[(j, m)])
                b.op("vector", lambda q, _par=par, _m=m: q.tensor_scalar(
                    wq3[:, _par, _m, :], wf3[:, _m, :, :].rearrange("p k f -> p (k f)"),
                    t_sb[:, _m:_m + 1], 2.0, ALU.is_ge, ALU.mult))
                b.op("vector", lambda q, _m=m: q.tensor_scalar(
                    t_stbf[:, 2 * HALF:4 * HALF],
                    wf3[:, _m, :, :].rearrange("p k f -> p (k f)"),
                    t_sb[:, 4 + _m:5 + _m], 2.0, ALU.is_le, ALU.mult))
                ev_q[(j, m)] = b.op("vector", lambda q, _par=par, _m=m: q.tensor_tensor(
                    wq3[:, _par, _m, :], wq3[:, _par, _m, :],
                    t_stbf[:, 2 * HALF:4 * HALF], ALU.subtract), sem=sV)

        def emit_store(j):
            if kind != "quant" or j >= NFB:
                return
            par = j % 2
            for m in (0, 1):
                b.wait("scalar", ev_q[(j, m)])
                ev_sdma[(j, m)] = b.op(
                    "scalar", lambda q, _par=par, _m=m, _j=j: q.dma_start(
                        out=wdst[_m][_j], in_=wq3[:, _par, _m, :]),
                    sem=sWSQ[2 * par + m], inc=16)

        emit_dma(0)
        emit_quant(0)
        emit_dma(1)
        emit_quant(1)
        emit_store(0)
        emit_store(1)
        for fb in range(NFB):
            par = fb % 2
            wready = ev_q[(fb, 1)] if kind == "quant" else ev_d[(fb, 1)]
            b.wait("tensor", wready)
            if kind != "quant":
                b.wait("tensor", ev_d[(fb, 0)])
            if fb == 0:
                b.wait("tensor", ev_ln)
            if fb >= 2:
                b.wait("tensor", ev_evac_v[fb - 2])
                b.wait("tensor", ev_evac_s[fb - 2])
            x1b = (0, 1) if par == 0 else (4, 5)
            x2b = (2, 3) if par == 0 else (6, 7)
            ev = None
            for k in range(NDT):
                for m, bks in ((0, x1b), (1, x2b)):
                    for ci, c in enumerate((0, 512)):
                        last = (k == NDT - 1 and m == 1 and ci == 1)
                        ev = b.op("tensor", lambda q, _par=par, _k=k, _c=c, _m=m,
                                  _b0=bks[ci], _wv=wv:
                                  q.matmul(pbank(_b0),
                                           _wv[:, _par, _m, _k * 128:(_k + 1) * 128],
                                           ln3[:, _k, _c:_c + 512],
                                           start=(_k == 0), stop=(_k == NDT - 1)),
                                  sem=sT if last else None)
            grp_ev[fb] = ev
            emit_dma(fb + 2)
            emit_quant(fb + 2)
            b.wait("scalar", grp_ev[fb])
            ev_evac_s[fb] = b.op("scalar", lambda q, _b0=x1b[0]: q.activation(
                t_tmp1[:], psA[:, _b0 * 512:_b0 * 512 + 1024], AF.Sigmoid,
                bias=t_zero[:, 0:1], scale=t_onesf[:, 0:1]), sem=sA)
            b.wait("vector", ev_evac_s[fb])
            ev_evac_v[fb] = b.op("vector", lambda q, _b0=x2b[0], _fb=fb:
                                 q.scalar_tensor_tensor(
                hid3[:, _fb, :], psA[:, _b0 * 512:_b0 * 512 + 1024], 0.25,
                t_tmp1[:], ALU.mult, ALU.mult), sem=sV)
            emit_store(fb + 2)
        return ev_evac_v[NFB - 1]

    def w3_stage(kind, half, x_src, x_dst, ev_hid, gate_reads=()):
        c0 = half * HALF
        grp_ev = [None] * NDT
        ev_ev = [None] * NDT
        ev_xodma = [None] * NDT
        ev_xin = [None] * NDT
        ev_q = {}
        ev_d = {}
        ev_w3dma = {}
        osem = sO if x_dst is d_out else sXO
        xout_b = (t_xout, t_xout2q if kind == "quant" else t_xout2s)
        NCHK = 4
        CW = 11 * 128

        def emit_dq(j):
            if kind != "quant" or j >= NDT:
                return
            par = j % 2
            for cc in range(NCHK):
                sl = cc % 2
                if j == 0 and cc == 0:
                    b.wait("sync", ev_hid)
                elif cc >= 2:
                    b.wait("sync", ev_q[(j, cc - 2)])
                elif j >= 1:
                    b.wait("sync", ev_q[(j - 1, cc + 2)])
                ev_d[(j, cc)] = b.op("sync", lambda q, _j=j, _cc=cc, _sl=sl: q.dma_start(
                    out=t_w3f[:, _sl * CW:(_sl + 1) * CW]
                    .rearrange("p (k e) -> p k e", k=11),
                    in_=d_w3T[_cc * 1408:(_cc + 1) * 1408, _j * 128:(_j + 1) * 128]
                    .rearrange("(k p) e -> p k e", p=128)), sem=sW3F[sl], inc=16)
                if j >= 2 and cc == 0:
                    b.wait("vector", grp_ev[j - 2])
                    b.wait("vector", ev_w3dma[j - 2])
                b.wait("vector", ev_d[(j, cc)])
                b.op("vector", lambda q, _par=par, _cc=cc, _sl=sl: q.tensor_scalar(
                    w3q3[:, _par, _cc * CW:(_cc + 1) * CW],
                    t_w3f[:, _sl * CW:(_sl + 1) * CW],
                    t_sb[:, 2:3], 2.0, ALU.is_ge, ALU.mult))
                b.op("vector", lambda q, _sl=sl: q.tensor_scalar(
                    t_w3tmp[:, 0:CW], t_w3f[:, _sl * CW:(_sl + 1) * CW],
                    t_sb[:, 6:7], 2.0, ALU.is_le, ALU.mult))
                ev_q[(j, cc)] = b.op("vector", lambda q, _par=par, _cc=cc: q.tensor_tensor(
                    w3q3[:, _par, _cc * CW:(_cc + 1) * CW],
                    w3q3[:, _par, _cc * CW:(_cc + 1) * CW],
                    t_w3tmp[:, 0:CW], ALU.subtract), sem=sV)

        def emit_store(j):
            if kind != "quant" or j >= NDT:
                return
            par = j % 2
            b.wait("scalar", ev_q[(j, NCHK - 1)])
            ev_w3dma[j] = b.op("scalar", lambda q, _par=par, _j=j: q.dma_start(
                out=d_w3q[_j], in_=w3q3[:, _par, :]), sem=sW3SQ[par], inc=16)

        def emit_rdma(j):
            if kind == "quant" or j >= NDT:
                return
            par = j % 2
            if j >= 2:
                b.wait("sync", grp_ev[j - 2])
            elif j == 0:
                b.wait("sync", ev_hid)
                for ev in gate_reads:
                    b.wait("sync", ev)
            ev_d[(j, 0)] = b.op("sync", lambda q, _par=par, _j=j: q.dma_start(
                out=w3q3[:, _par, :], in_=d_w3q[_j]), sem=sW3R[par], inc=16)

        if kind == "quant":
            emit_dq(0)
            emit_dq(1)
            emit_store(0)
            emit_store(1)
        else:
            emit_rdma(0)
            emit_rdma(1)
        for eb in range(NDT):
            par = eb % 2
            if eb == 0:
                b.wait("scalar", ev_hid)
            else:
                b.wait("scalar", ev_ev[eb - 1])
            ev_xin[eb] = b.op("scalar", lambda q, _eb=eb: q.dma_start(
                out=t_tmp1[:], in_=x_src[_eb * 128:(_eb + 1) * 128, c0:c0 + HALF]),
                sem=sXIN, inc=16)
            wready = ev_q[(eb, NCHK - 1)] if kind == "quant" else ev_d[(eb, 0)]
            b.wait("tensor", wready)
            if eb == 0:
                b.wait("tensor", ev_hid)
            if eb >= 2:
                b.wait("tensor", ev_ev[eb - 2])
            bk = 0 if par == 0 else 2
            for k in range(NFB):
                for ci, c in enumerate((0, 512)):
                    last = (k == NFB - 1 and ci == 1)
                    grp_ev[eb] = b.op("tensor", lambda q, _par=par, _k=k, _c=c, _bk=bk + ci:
                                      q.matmul(pbank(_bk),
                                               w3q3[:, _par, _k * 128:(_k + 1) * 128],
                                               hid3[:, _k, _c:_c + 512],
                                               start=(_k == 0), stop=(_k == NFB - 1)),
                                      sem=sT if last else None)
            if kind == "quant":
                emit_dq(eb + 2)
            else:
                emit_rdma(eb + 2)
            b.wait("vector", grp_ev[eb])
            b.wait("vector", ev_xin[eb])
            if eb > 1:
                b.wait("vector", ev_xodma[eb - 2])
            ev_ev[eb] = b.op("vector", lambda q, _bk=bk, _xo=xout_b[eb % 2]:
                             q.tensor_tensor(
                _xo[:], psA[:, _bk * 512:_bk * 512 + 1024], t_tmp1[:], ALU.add), sem=sV)
            b.wait("scalar", ev_ev[eb])
            ev_xodma[eb] = b.op("scalar", lambda q, _eb=eb, _xo=xout_b[eb % 2]:
                                q.dma_start(
                out=x_dst[_eb * 128:(_eb + 1) * 128, c0:c0 + HALF], in_=_xo[:]),
                sem=osem[eb % 2], inc=16)
            if kind == "quant":
                emit_store(eb + 2)
        last_q = ev_q.get((NDT - 1, NCHK - 1))
        return ev_ev[NDT - 1], ev_xodma[NDT - 1], grp_ev[NDT - 1], last_q

    def proj(ev_ln2_done, wc_gate=None):
        ev_grp = {}
        ev_evac = {}
        ev_edma = {}
        mats = ("q", "k", "v")
        seq = [(mm_, eb) for mm_ in mats for eb in range(NDT)]
        for i, (mm_, eb) in enumerate(seq):
            par = i % 2
            if i == 0 and wc_gate is not None:
                b.wait("gpsimd", wc_gate)
            if i >= 2:
                b.wait("gpsimd", ev_grp[seq[i - 2]])
            ev_w = b.op("gpsimd", lambda q, _m=mm_, _eb=eb, _par=par: q.dma_start(
                out=wc3[:, _par, :].rearrange("p (k e) -> p k e", k=16),
                in_=d_W[_m][:, _eb * 128:(_eb + 1) * 128]
                .rearrange("(k p) e -> p k e", p=128)), sem=sWC[par], inc=16)
            b.wait("tensor", ev_w)
            if i == 0:
                b.wait("tensor", ev_ln2_done)
            if i >= 2:
                b.wait("tensor", ev_evac[seq[i - 2]])
            b0 = 0 if par == 0 else 4
            ev = None
            for ci in range(4):
                for k in range(NDT):
                    last = (ci == 3 and k == NDT - 1)
                    ev = b.op("tensor", lambda q, _par=par, _k=k, _ci=ci, _b0=b0:
                              q.matmul(pbank(_b0 + _ci),
                                       wc3[:, _par, _k * 128:(_k + 1) * 128],
                                       ln23[:, _k, _ci * 512:(_ci + 1) * 512],
                                       start=(_k == 0), stop=(_k == NDT - 1)),
                              sem=sT if last else None)
            ev_grp[(mm_, eb)] = ev
            b.wait("vector", ev)
            if i == 0:
                b.wait("vector", E["init"])
            if i >= 2:
                b.wait("vector", ev_edma[seq[i - 2]])
            pe = t_pe0 if par == 0 else t_pe1
            ev_evac[(mm_, eb)] = b.op("vector", lambda q, _m=mm_, _eb=eb, _pe=pe, _b0=b0:
                                      q.tensor_scalar_add(
                _pe[:], psA[:, _b0 * 512:(_b0 + 4) * 512],
                t_bias[_m][:, _eb:_eb + 1]), sem=sV)
            b.wait("scalar", ev_evac[(mm_, eb)])
            ev_edma[(mm_, eb)] = b.op("scalar", lambda q, _m=mm_, _eb=eb, _pe=pe:
                                      q.dma_start(
                out=d_sc[_m][_eb * 128:(_eb + 1) * 128, :], in_=_pe[:]), sem=sPE[par], inc=16)
        E["proj_last_mm"] = ev_grp[seq[-1]]
        E["proj_pe0"] = (sPE[0].h, sPE[0].n)
        E["proj_pe1"] = (sPE[1].h, sPE[1].n)

    chunk_tensor_done = [None] * NCH

    def scan():
        b.wait("vector", E["proj_last_mm"])
        b.op("vector", lambda q: q.memset(t_Sf[:], 0.0))
        ev_s0 = b.op("vector", lambda q: q.memset(t_Sb[:], 0.0), sem=sV)
        at_evac_ev = [None, None]
        out_free_ev = None
        su_free_ev = None
        out_mm_prev = None
        xdma_ev = [None] * NCH
        x2dma_ev = [None] * NCH
        chunk_done_v = [None] * NCH
        suev = None
        su_mm_ev = None
        for c in range(NCH):
            par = c % 2
            col = c * CH
            if c < 2:
                b.wait("sync", E["proj_pe0"])
                b.wait("sync", E["proj_pe1"])
            else:
                b.wait("sync", chunk_tensor_done[c - 2])
            ev_qc = b.op("sync", lambda q, _par=par, _col=col: q.dma_start(
                out=t_qc[_par][:].rearrange("p (k t) -> p k t", k=16),
                in_=d_sc["q"][:, _col:_col + CH].rearrange("(k p) t -> p k t", p=128)),
                sem=sQC[par], inc=16)
            ev_kc = b.op("sync", lambda q, _par=par, _col=col: q.dma_start(
                out=t_kc[_par][:].rearrange("p (k t) -> p k t", k=16),
                in_=d_sc["k"][:, _col:_col + CH].rearrange("(k p) t -> p k t", p=128)),
                sem=sKC[par], inc=16)
            ev_kn = ev_vn = None
            for jb in (0, 1):
                ev_kn = b.op("sync", lambda q, _par=par, _col=col, _jb=jb: q.dma_start(
                    out=kn3[_par][:, _jb, :],
                    in_=d_sc["k"][:, _col + _jb * 128:_col + (_jb + 1) * 128],
                    transpose=True), sem=sKN[par], inc=16)
                ev_vn = b.op("sync", lambda q, _par=par, _col=col, _jb=jb: q.dma_start(
                    out=vn3[_par][:, _jb, :],
                    in_=d_sc["v"][:, _col + _jb * 128:_col + (_jb + 1) * 128],
                    transpose=True), sem=sVN[par], inc=16)
            if c >= 1:
                b.wait("scalar", chunk_done_v[c - 1])
            xdma_ev[c] = b.op("scalar", lambda q, _col=col: q.dma_start(
                out=xc3, in_=d_xp[:, _col:_col + CH]
                .rearrange("(k p) t -> p k t", p=128)), sem=sXC, inc=16)
            b.wait("vector", ev_kn)
            if c == 0:
                b.wait("vector", E["attn_consts"])
                b.wait("vector", ev_s0)
            for jb in (0, 1):
                b.op("vector", lambda q, _par=par, _jb=jb: q.tensor_tensor(
                    kn3[_par][:, _jb, :], kn3[_par][:, _jb, :], kst3[:, _jb, :], ALU.mult),
                    sem=sV)
            for h in range(HEADS):
                hp = h % 2
                ab0 = 0 if hp == 0 else 2
                b.wait("vector", out_mm_prev)
                if h == 0:
                    b.wait("vector", ev_qc)
                qev = None
                for kk in (0, 1):
                    qev = b.op("vector", lambda q, _par=par, _h=h, _kk=kk: q.tensor_tensor(
                        qsh2[:, _kk, :], qc3[_par][:, 2 * _h + _kk, :],
                        t_cs[:, _h * CH:(_h + 1) * CH], ALU.mult), sem=sV)
                b.wait("tensor", qev)
                if h == 0:
                    b.wait("tensor", ev_kc)
                b.wait("tensor", at_evac_ev[hp])
                ev = None
                for jb in (0, 1):
                    for kk in (0, 1):
                        ev = b.op("tensor", lambda q, _par=par, _h=h, _jb=jb, _kk=kk,
                                  _b0=ab0: q.matmul(
                            pbank(_b0 + _jb, 256),
                            kc3[_par][:, 2 * _h + _kk, _jb * 128:(_jb + 1) * 128],
                            qc3[_par][:, 2 * _h + _kk, :],
                            start=(_kk == 0), stop=(_kk == 1)),
                            sem=sT if (jb == 1 and kk == 1) else None)
                b.wait("vector", ev)
                aev = None
                for jb in (0, 1):
                    aev = b.op("vector", lambda q, _jb=jb, _h=h, _b0=ab0: q.tensor_tensor(
                        at2[:, _jb, :], pbank(_b0 + _jb, 256),
                        dt3[:, _jb, _h * CH:(_h + 1) * CH], ALU.mult), sem=sV)
                at_evac_ev[hp] = aev
                b.wait("tensor", aev)
                b.wait("tensor", out_free_ev)
                if h == 0:
                    b.wait("tensor", ev_vn)
                    if c >= 1:
                        b.wait("tensor", chunk_done_v[c - 1])
                ev = None
                for eb in (0, 1):
                    for jb in (0, 1):
                        b.op("tensor", lambda q, _par=par, _h=h, _eb=eb, _jb=jb: q.matmul(
                            pbank(4 + _eb, 256),
                            vn3[_par][:, _jb, _h * DH + _eb * 128:_h * DH + (_eb + 1) * 128],
                            at2[:, _jb, :], start=(_jb == 0), stop=False))
                    for kk in (0, 1):
                        ev = b.op("tensor", lambda q, _h=h, _eb=eb, _kk=kk: q.matmul(
                            pbank(4 + _eb, 256),
                            Sb3[:, _kk, _h, _eb * 128:(_eb + 1) * 128],
                            qsh2[:, _kk, :], start=False, stop=(_kk == 1)),
                            sem=sT if (eb == 1 and kk == 1) else None)
                out_mm_ev = ev
                out_mm_prev = ev
                b.wait("tensor", su_free_ev)
                ev = None
                for kk in (0, 1):
                    for jb in (0, 1):
                        ev = b.op("tensor", lambda q, _par=par, _h=h, _kk=kk, _jb=jb:
                                  q.matmul(
                            pbank(6 + _kk, 256),
                            kn3[_par][:, _jb, _h * DH + _kk * 128:_h * DH + (_kk + 1) * 128],
                            vn3[_par][:, _jb, _h * DH:(_h + 1) * DH],
                            start=(_jb == 0), stop=(_jb == 1)),
                            sem=sT if (kk == 1 and jb == 1) else None)
                su_mm_ev = ev
                b.wait("vector", out_mm_ev)
                b.wait("vector", xdma_ev[c])
                if c >= 1 and h == 0:
                    b.wait("vector", x2dma_ev[c - 1])
                oev = None
                for eb in (0, 1):
                    oev = b.op("vector", lambda q, _h=h, _eb=eb: q.tensor_tensor(
                        x23[:, 2 * _h + _eb, :], pbank(4 + _eb, 256),
                        xc3[:, 2 * _h + _eb, :], ALU.add), sem=sV)
                out_free_ev = oev
                b.wait("vector", su_mm_ev)
                for kk in (0, 1):
                    b.op("vector", lambda q, _h=h, _kk=kk: q.scalar_tensor_tensor(
                        Sf3[:, _kk, _h, :], Sf3[:, _kk, _h, :], CD[_h],
                        pbank(6 + _kk, 256), ALU.mult, ALU.add))
                    suev = b.op("vector", lambda q, _h=h, _kk=kk: q.tensor_copy(
                        Sb3[:, _kk, _h, :], Sf3[:, _kk, _h, :]), sem=sV)
                su_free_ev = suev
            chunk_done_v[c] = suev
            chunk_tensor_done[c] = su_mm_ev
            b.wait("scalar", out_free_ev)
            x2dma_ev[c] = b.op("scalar", lambda q, _col=col: q.dma_start(
                out=d_xpp[:, _col:_col + CH].rearrange("(k p) t -> p k t", p=128),
                in_=x23), sem=sX2, inc=16)
        b.wait("gpsimd", chunk_done_v[NCH - 1])
        ev_ai = b.op("gpsimd", lambda q: q.dma_start(
            out=d_agin.rearrange("(k p) e -> p k e", p=128),
            in_=t_Sf[:].rearrange("p (k e) -> p k e", k=16)), sem=sCC, inc=16)
        b.wait("gpsimd", ev_ai)
        E["ag_done"] = b.op("gpsimd", lambda q: q.collective_compute(
            "AllGather", ALU.bypass,
            replica_groups=[[0, 1], [2, 3], [4, 5], [6, 7]],
            ins=[d_agin.opt()], outs=[d_agout.opt()]), sem=sCC, inc=1)
        E["scan_x2_last"] = x2dma_ev[NCH - 1]

    corr_t_done = [None] * NCH

    def corr():
        b.wait("gpsimd", E["ag_done"])
        ev_b0 = b.op("gpsimd", lambda q: q.dma_start(
            out=t_T[:].rearrange("p (k e) -> p k e", k=16),
            in_=d_agout[0:DM, :].rearrange("(k p) e -> p k e", p=128)), sem=sCC, inc=16)
        b.wait("gpsimd", E["scan_x2_last"])
        b.wait("gpsimd", ev_b0)
        ev_b1 = b.op("gpsimd", lambda q: q.dma_start(
            out=t_x2[:].rearrange("p (k e) -> p k e", k=16),
            in_=d_agout[DM:2 * DM, :].rearrange("(k p) e -> p k e", p=128)), sem=sCC, inc=16)
        b.wait("vector", ev_b1)
        b.op("vector", lambda q: q.tensor_scalar_mul(t_T[:], t_T[:], t_msk[:, 0:1]))
        ev_T = b.op("vector", lambda q: q.scalar_tensor_tensor(
            t_T[:], t_x2[:], t_msk[:, 1:2], t_T[:], ALU.mult, ALU.add), sem=sV)
        b.wait("vector", ev_T)
        ev_tb = b.op("vector", lambda q: q.tensor_copy(t_Tb[:], t_T[:]), sem=sV)
        prev_xdma = None
        last_t = None
        oev2 = [None, None]
        qev = None
        for c in range(NCH):
            par = c % 2
            col = c * CH
            if c >= 2:
                b.wait("sync", corr_t_done[c - 2])
            else:
                b.wait("sync", E["scan_x2_last"])
            ev_q = b.op("sync", lambda q, _par=par, _col=col: q.dma_start(
                out=t_qc[_par][:].rearrange("p (k t) -> p k t", k=16),
                in_=d_sc["q"][:, _col:_col + CH].rearrange("(k p) t -> p k t", p=128)),
                sem=sQC[par], inc=16)
            if c == 0:
                b.wait("scalar", E["scan_x2_last"])
            if prev_xdma is not None:
                b.wait("scalar", prev_xdma)
            ev_x = b.op("scalar", lambda q, _col=col: q.dma_start(
                out=xc3, in_=d_xpp[:, _col:_col + CH]
                .rearrange("(k p) t -> p k t", p=128)), sem=sXC, inc=16)
            oev = None
            for h in range(HEADS):
                hp = h % 2
                if h == 0:
                    b.wait("vector", ev_q)
                b.wait("vector", last_t)
                cdc = float(CD[h] ** c)
                for kk in (0, 1):
                    qev = b.op("vector", lambda q, _par=par, _h=h, _kk=kk, _s=cdc:
                               q.scalar_tensor_tensor(
                        qsh2[:, _kk, :], qc3[_par][:, 2 * _h + _kk, :], _s,
                        t_cs[:, _h * CH:(_h + 1) * CH], ALU.mult, ALU.mult), sem=sV)
                b.wait("tensor", qev)
                if h == 0 and c == 0:
                    b.wait("tensor", ev_tb)
                if oev2[hp] is not None:
                    b.wait("tensor", oev2[hp])
                for eb in (0, 1):
                    for kk in (0, 1):
                        last_t = b.op("tensor", lambda q, _h=h, _eb=eb, _kk=kk, _hp=hp:
                                      q.matmul(
                            pbank(2 * _hp + _eb, 256),
                            Tb3[:, _kk, _h, _eb * 128:(_eb + 1) * 128],
                            qsh2[:, _kk, :], start=(_kk == 0), stop=(_kk == 1)),
                            sem=sT if (eb == 1 and kk == 1) else None)
                b.wait("vector", last_t)
                if h == 0:
                    b.wait("vector", ev_x)
                for eb in (0, 1):
                    oev = b.op("vector", lambda q, _h=h, _eb=eb, _hp=hp: q.tensor_tensor(
                        xc3[:, 2 * _h + _eb, :], pbank(2 * _hp + _eb, 256),
                        xc3[:, 2 * _h + _eb, :], ALU.add), sem=sV)
                oev2[hp] = oev
            corr_t_done[c] = last_t
            b.wait("scalar", oev)
            prev_xdma = b.op("scalar", lambda q, _col=col: q.dma_start(
                out=d_xpp[:, _col:_col + CH].rearrange("(k p) t -> p k t", p=128),
                in_=xc3), sem=sXC, inc=16)
        E["corr_last_xdma"] = prev_xdma
        E["corr_last_qev"] = qev

    if TESTMODE:
        if TESTMODE == 1:
            ph_init()
        else:
            b.op("gpsimd", lambda q: q.dma_start(out=t_msk[:], in_=d_msk), sem=sInit, inc=16)
            E["init"] = b.op("gpsimd", lambda q: q.dma_start(
                out=t_bias["v"][:], in_=d_b["v"]), sem=sInit, inc=16)
            b.op("vector", lambda q: q.memset(t_ones[:], 1.0))
            b.op("vector", lambda q: q.memset(t_onesf[:], 1.0))
            b.op("vector", lambda q: q.memset(t_hm[:], -0.5))
            b.op("vector", lambda q: q.memset(t_zero[:], 0.0))
            E["zeros"] = b.op("vector", lambda q: q.memset(t_hp[:], 0.5), sem=sV)
            E["s_ready"] = E["zeros"]
        ev_ln = ln_block(d_x, 0, 0.5, ln3, 0)
        b.wait("gpsimd", ev_ln)
        ev_f = b.op("gpsimd", lambda q: q.dma_start(out=d_dbgl[0], in_=t_ln[:]),
                    sem=sDBG, inc=16)
        b.wait("gpsimd", ev_f)
        b.wait("scalar", ev_f)
        b.wait("sync", ev_f)
        return b.finish()

    ph_init()

    ev_ln = ln_block(d_x, 0, 0.5, ln3, 0)
    ph_init_tail()
    if DEBUG:
        b.wait("gpsimd", ev_ln)
        b.op("gpsimd", lambda q: q.dma_start(out=d_dbgl[0], in_=t_ln[:]), sem=sDBG, inc=16)
    ev_hid = x1x2("quant", ev_ln)
    if DEBUG:
        b.wait("gpsimd", ev_hid)
        b.op("gpsimd", lambda q: q.dma_start(out=d_dbgh[0], in_=t_hid[:]), sem=sDBG, inc=16)
    w3a = w3_stage("quant", 0, d_x, d_xp, ev_hid)
    ev_ln = ln_block(d_x, 1, 0.5, ln3, 0, wait_load=[("scalar", w3a[2])])
    sc_gates = [(sWSQ[i].h, sWSQ[i].n) for i in range(4)]
    ev_hid = x1x2("scratch", ev_ln, wait_w=[w3a[2], w3a[1]], gate_reads=sc_gates)
    if DEBUG:
        b.wait("gpsimd", ev_hid)
        b.op("gpsimd", lambda q: q.dma_start(out=d_dbgh[1], in_=t_hid[:]), sem=sDBG, inc=16)
        b.op("gpsimd", lambda q: q.dma_start(out=d_dbgl[1], in_=t_ln[:]), sem=sDBG, inc=16)
    w3_gates = [(sW3SQ[i].h, sW3SQ[i].n) for i in range(2)]
    w3b = w3_stage("scratch", 1, d_x, d_xp, ev_hid, gate_reads=w3_gates)

    ev_l2a = ln_block(d_xp, 0, 1.0, ln23, 0,
                      wait_load=[("scalar", w3b[2]),
                                 ("scalar", (sXO[0].h, sXO[0].n)),
                                 ("scalar", (sXO[1].h, sXO[1].n))])
    ev_l2b = ln_block(d_xp, 1, 1.0, ln23, HALF, wait_load=[("scalar", ev_l2a)])

    proj(ev_l2b, wc_gate=w3a[1])

    b.wait("gpsimd", ev_l2b)
    b.op("gpsimd", lambda q: q.dma_start(
        out=dt3, in_=d_dt.rearrange("(j p) n -> p j n", p=128)), sem=sTabs, inc=16)
    b.op("gpsimd", lambda q: q.dma_start(out=t_cs[:], in_=d_cs), sem=sTabs, inc=16)
    E["attn_consts"] = b.op("gpsimd", lambda q: q.dma_start(
        out=kst3, in_=d_kst.rearrange("(j p) n -> p j n", p=128)), sem=sTabs, inc=16)

    scan()
    corr()

    ev_ln = ln_block(d_xpp, 0, 0.5, ln3, 0,
                     wait_load=[("scalar", E["corr_last_xdma"])])
    ev_hid = x1x2("scratch", ev_ln, wait_w=[E["corr_last_qev"]], gate_reads=sc_gates)
    w3f2 = w3_stage("scratch", 0, d_xpp, d_out, ev_hid, gate_reads=w3_gates)
    ev_ln = ln_block(d_xpp, 1, 0.5, ln3, 0, wait_load=[("scalar", w3f2[2])])
    ev_hid = x1x2("scratch", ev_ln, wait_w=[w3f2[2], w3f2[1]])
    w3g = w3_stage("scratch", 1, d_xpp, d_out, ev_hid)

    b.wait("scalar", (sO[0].h, sO[0].n))
    b.wait("scalar", (sO[1].h, sO[1].n))
    b.wait("sync", (sO[0].h, sO[0].n))
    b.wait("sync", (sO[1].h, sO[1].n))

    return b.finish()


def _host_tables():
    idx = np.arange(CH)
    dt = np.zeros((CH, HEADS * CH), np.float32)
    cs = np.zeros((128, HEADS * CH), np.float32)
    kst = np.zeros((CH, DM), np.float32)
    for h in range(HEADS):
        g = _g[h]
        D = np.where(idx[:, None] >= idx[None, :],
                     g ** (idx[:, None] - idx[None, :]).astype(np.float64), 0.0)
        dt[:, h * CH:(h + 1) * CH] = D.T.astype(np.float32)
        cs[:, h * CH:(h + 1) * CH] = np.broadcast_to(
            (g ** (idx + 1).astype(np.float64)).astype(np.float32), (128, CH))
        kst[:, h * DH:(h + 1) * DH] = np.broadcast_to(
            (g ** (CH - 1 - idx).astype(np.float64)).astype(np.float32)[:, None],
            (CH, DH))
    return dt, cs, kst


_CACHE = {}


def kernel(L0_attn_x, L0_attn_Wq, L0_attn_bq, L0_attn_Wk, L0_attn_bk,
           L0_attn_Wv, L0_attn_bv, L0_ffn_w1, L0_ffn_w2, L0_ffn_w3):
    x = np.ascontiguousarray(np.asarray(L0_attn_x, np.float32))
    Wq = np.ascontiguousarray(np.asarray(L0_attn_Wq, np.float32))
    Wk = np.ascontiguousarray(np.asarray(L0_attn_Wk, np.float32))
    Wv = np.ascontiguousarray(np.asarray(L0_attn_Wv, np.float32))
    bq = np.asarray(L0_attn_bq, np.float32)
    bk = np.asarray(L0_attn_bk, np.float32)
    bv = np.asarray(L0_attn_bv, np.float32)
    w1 = np.asarray(L0_ffn_w1, np.float32)
    w2 = np.asarray(L0_ffn_w2, np.float32)
    w3 = np.asarray(L0_ffn_w3, np.float32)

    w1T = np.ascontiguousarray(w1.T)
    w2T = np.ascontiguousarray(w2.T)
    w3T = np.ascontiguousarray(w3.T)

    dt, cs, kst = _host_tables()
    bq_r = np.ascontiguousarray(bq.reshape(16, 128).T)
    bk_r = np.ascontiguousarray(bk.reshape(16, 128).T)
    bv_r = np.ascontiguousarray(bv.reshape(16, 128).T)

    if "nc" not in _CACHE:
        _CACHE["nc"] = build_program()
    nc = _CACHE["nc"]

    in_maps = []
    for c in range(8):
        bb, hh = divmod(c, 2)
        xT = np.ascontiguousarray(x[bb, hh * TOK:(hh + 1) * TOK, :].T)
        wsh = np.empty((3, 128, 11264), np.float32)
        wsh[0] = w1T[256 * c:256 * (c + 1), :].reshape(2, 128, FH) \
            .transpose(1, 0, 2).reshape(128, 2 * FH)
        wsh[1] = w2T[256 * c:256 * (c + 1), :].reshape(2, 128, FH) \
            .transpose(1, 0, 2).reshape(128, 2 * FH)
        wsh[2] = w3T[:, 256 * c:256 * (c + 1)].reshape(NFB, 128, 256) \
            .transpose(1, 0, 2).reshape(128, NFB * 256)
        cmask = np.zeros((128, 2), np.float32)
        if hh == 1:
            cmask[:, 0] = 1.0
        in_maps.append({
            "xT": xT, "w1T": w1T, "w2T": w2T, "w3T": w3T,
            "Wq": Wq, "Wk": Wk, "Wv": Wv, "wsh": wsh,
            "bq_r": bq_r, "bk_r": bk_r, "bv_r": bv_r,
            "dt_tab": dt, "cs_tab": cs, "ks_tab": kst, "cmask": cmask,
        })

    res = run_bass_kernel_spmd(nc, in_maps, core_ids=list(range(8)))
    _CACHE["res"] = res
    out = np.empty((4, 4096, DM), np.float32)
    for c in range(8):
        bb, hh = divmod(c, 2)
        out[bb, hh * TOK:(hh + 1) * TOK, :] = res.results[c]["outT"].T
    return out
